# version 16 (final)
# baseline (speedup 1.0000x reference)
import numpy as np
import concourse.bass as bass
import concourse.mybir as mybir
from concourse import library_config

F32 = mybir.dt.float32
F32R = mybir.dt.float32r
F16 = mybir.dt.float16
ALU = mybir.AluOpType
AF = mybir.ActivationFunctionType
AX = mybir.AxisListType


class Cfg:
    def __init__(self, D=1024, I=2816, S=4096, TA=512, TF=1024, TB=1008, eps_blk=1e-5,
                 eps_l0=1e-6):
        self.D = D
        self.I = I
        self.S = S
        self.TA = TA
        self.TF = TF
        self.TB = TB
        self.NH = D // 64
        self.DH = 64
        self.eps_blk = eps_blk
        self.eps_l0 = eps_l0
        self.DC = D // 128
        self.IC = I // 128
        self.NB = D // 16
        self.IB = I // 16
        assert D % 128 == 0 and I % 128 == 0 and S % TA == 0 and S % TF == 0
        assert TB % 126 == 0
        self.W = (S - 4) // 2 + 1
        self.NCH = (self.W + 62) // 63



FCONST_VALS = (1e-5, 1e-6, 0.03125, 2.0, 0.5, 3.0, 4.0, 6.0)

_DEC32_B64 = "8wQ1P+8zNT+xYjU/OpE1P4m/NT+f7TU/fRs2PyNJNj+RdjY/yKM2P8jQNj+S/TY/JSo3P4NWNz+sgjc/oK43P17aNz/qBTg/QTE4P2RcOD9Uhzg/ErI4P53cOD/2Bjk/HTE5PxJbOT/XhDk/aq45P87XOT8BATo/BCo6P9hSOj99ezo/86M6PzrMOj9T9Do/Phw7P/xDOz+Mazs/75I7Pya6Oz8w4Ts/Dgg8P8AuPD9HVTw/ons8P9KhPD/Yxzw/s+08P2QTPT/rOD0/SV49P32DPT+IqD0/as09PyTyPT+1Fj4/Hjs+P2BfPj96gz4/bKc+PzjLPj/d7j4/WxI/P7M1Pz/lWD8/8Xs/P9eePz+YwT8/NOQ/P6sGQD/9KEA/K0tAPzRtQD8aj0A/27BAP3nSQD/080A/TBVBP4E2QT+TV0E/gnhBP1CZQT/7uUE/hNpBP+z6QT8yG0I/VjtCP1pbQj89e0I//5pCP6G6Qj8i2kI/g/lCP8UYQz/nN0M/6VZDP8t1Qz+PlEM/NLNDP7nRQz8g8EM/aQ5EP5QsRD+gSkQ/jmhEP1+GRD8SpEQ/qMFEPyDfRD98/EQ/uhlFP9w2RT/hU0U/ynBFP5aNRT9HqkU/28ZFP1TjRT+x/0U/8htGPxk4Rj8kVEY/FHBGP+mLRj+kp0Y/RMNGP8reRj81+kY/hhVHP70wRz/bS0c/32ZHP8mBRz+anEc/UbdHP/DRRz917Ec/4gZIPzYhSD9xO0g/lFVIP55vSD+RiUg/a6NIPy69SD/Y1kg/a/BIP+cJST9LI0k/lzxJP81VST/rbkk/84dJP+SgST++uUk/gdJJPy7rST/FA0o/RhxKP7A0Sj8FTUo/Q2VKP2x9Sj9/lUo/fa1KP2XFSj853Uo/9vRKP58MSz8zJEs/sjtLPxxTSz9yaks/s4FLP+CYSz/4r0s//MZLP+zdSz/I9Es/kAtMP0QiTD/lOEw/cU9MP+tlTD9RfEw/o5JMP+OoTD8Pv0w/KNVMPy/rTD8iAU0/AxdNP9EsTT+NQk0/NlhNP8xtTT9Rg00/w5hNPyOuTT9xw00/rdhNP9ftTT/wAk4/9xdOP+wsTj/QQU4/olZOP2NrTj8TgE4/spROPz+pTj+8vU4/J9JOP4LmTj/M+k4/Bg9PPy4jTz9HN08/T0tPP0ZfTz8tc08/BIdPP8uaTz+Crk8/KcJPP8DVTz9H6U8/vvxPPyYQUD9+I1A/xzZQPwBKUD8qXVA/RHBQP1CDUD9MllA/OalQPxa8UD/lzlA/puFQP1f0UD/6BlE/jRlRPxMsUT+JPlE/8VBRP0tjUT+WdVE/1IdRPwKaUT8jrFE/Nr5RPzrQUT8x4lE/GvRRP/UFUj/CF1I/gilSPzM7Uj/XTFI/bl5SP/dvUj9zgVI/4pJSP0OkUj+WtVI/3cZSPxfYUj9D6VI/Y/pSP3ULUz97HFM/dC1TP2A+Uz9AT1M/E2BTP9lwUz+SgVM/P5JTP+CiUz91s1M//MNTP3jUUz/o5FM/S/VTP6IFVD/tFVQ/LCZUP182VD+HRlQ/olZUP7FmVD+1dlQ/rYZUP5mWVD96plQ/T7ZUPxnGVD/X1VQ/iuVUPzH1VD/OBFU/XxRVP+QjVT9fM1U/zkJVPzJSVT+LYVU/2nBVPx2AVT9Vj1U/g55VP6atVT+9vFU/ystVP83aVT/F6VU/svhVP5UHVj9uFlY/OyVWP/8zVj+4QlY/Z1FWPwtgVj+lblY/NX1WP7uLVj83mlY/qKhWPxC3Vj9uxVY/wdNWPwviVj9L8FY/gf5WP60MVz/QGlc/6ChXP/c2Vz/9RFc/+VJXP+tgVz/Ublc/s3xXP4mKVz9VmFc/GKZXP9KzVz+CwVc/Kc9XP8fcVz9c6lc/5/dXP2oFWD/jElg/VCBYP7stWD8ZO1g/bkhYP7tVWD/+Ylg/OXBYP2t9WD+Uilg/tZdYP82kWD/csVg/4r5YP+DLWD/V2Fg/wuVYP6byWD+C/1g/VQxZPyAZWT/jJVk/nTJZP08/WT/5S1k/mlhZPzNlWT/EcVk/TX5ZP86KWT9Hl1k/t6NZPyCwWT+AvFk/2chZPyrVWT9y4Vk/s+1ZP+z5WT8eBlo/RxJaP2geWj+CKlo/lTZaP59CWj+iTlo/nVpaP5FmWj99clo/Yn5aPz+KWj8Ullo/4qFaP6mtWj9puVo/IMVaP9HQWj963Fo/HOhaP7fzWj9L/1o/1wpbP1wWWz/aIVs/US1bP8E4Wz8pRFs/i09bP+VaWz85Zls/hXFbP8t8Wz8KiFs/QZNbP3KeWz+cqVs/wLRbP9y/Wz/yyls/AdZbPwnhWz8K7Fs/BfdbP/kBXD/nDFw/zRdcP64iXD+HLVw/WjhcPydDXD/tTVw/rVhcP2ZjXD8Zblw/xXhcP2uDXD8Ljlw/pJhcPzejXD/ErVw/SrhcP8rCXD9EzVw/uNdcPyXiXD+N7Fw/7vZcP0kBXT+eC10/7RVdPzYgXT95Kl0/tTRdP+w+XT8dSV0/SFNdP2xdXT+LZ10/pXFdP7h7XT/FhV0/zY9dP86ZXT/Ko10/wK1dP7G3XT+bwV0/gMtdP1/VXT85310/DeldP9vyXT+j/F0/ZgZePyQQXj/cGV4/jiNePzstXj/iNl4/hEBePyBKXj+3U14/SF1eP9RmXj9acF4/3HleP1eDXj/OjF4/P5ZeP6ufXj8RqV4/crJeP867Xj8lxV4/ds5eP8PXXj8K4V4/TOpeP4jzXj/A/F4/8gVfPyAPXz9IGF8/ayFfP4kqXz+iM18/tjxfP8VFXz/PTl8/1FdfP9RgXz/QaV8/xnJfP7d7Xz+khF8/i41fP26WXz9Ln18/JKhfP/mwXz/IuV8/ksJfP1jLXz8Z1F8/1dxfP43lXz9A7l8/7vZfP5f/Xz88CGA/3BBgP3cZYD8OImA/oSpgPy4zYD+3O2A/PERgP7xMYD83VWA/rl1gPyBmYD+ObmA/93ZgP1x/YD+8h2A/GJBgP3CYYD/DoGA/EqlgP1yxYD+iuWA/5MFgPyHKYD9a0mA/jtpgP7/iYD/r6mA/EvNgPzb7YD9VA2E/cAthP4YTYT+ZG2E/pyNhP7ErYT+3M2E/uTthP7dDYT+wS2E/pVNhP5dbYT+EY2E/bWthP1JzYT8ye2E/D4NhP+iKYT+9kmE/jZphP1qiYT8jqmE/6LFhP6i5YT9lwWE/HslhP9PQYT+E2GE/MeBhP9vnYT+A72E/IfdhP7/+YT9ZBmI/7g1iP4EVYj8PHWI/mSRiPyAsYj+jM2I/IjtiP51CYj8VSmI/iFFiP/lYYj9lYGI/zmdiPzJvYj+UdmI/8X1iP0uFYj+ijGI/9JNiP0ObYj+PomI/1qliPxuxYj9buGI/mL9iP9LGYj8HzmI/OtViP2jcYj+U42I/vOpiP+DxYj8B+WI/HgBjPzgHYz9ODmM/YRVjP3AcYz98I2M/hSpjP4oxYz+LOGM/iT9jP4RGYz98TWM/cFRjP2FbYz9OYmM/OGljPx9wYz8Cd2M/431jP7+EYz+Zi2M/b5JjP0KZYz8SoGM/3qZjP6etYz9ttGM/MLtjP+/BYz+ryGM/ZM9jPxrWYz/N3GM/fONjPynqYz/S8GM/ePdjPxv+Yz+6BGQ/VwtkP/ARZD+HGGQ/Gh9kP6olZD83LGQ/wTJkP0g5ZD/MP2Q/TUZkP8pMZD9FU2Q/vVlkPzJgZD+jZmQ/Em1kP35zZD/meWQ/TIBkP6+GZD8PjWQ/bJNkP8aZZD8doGQ/caZkP8KsZD8Qs2Q/W7lkP6S/ZD/pxWQ/LMxkP2zSZD+o2GQ/495kPxrlZD9O62Q/gPFkP673ZD/a/WQ/AwRlPyoKZT9NEGU/bhZlP4wcZT+nImU/vyhlP9UuZT/nNGU/+DplPwVBZT8QR2U/GE1lPx1TZT8fWWU/H19lPxxlZT8Wa2U/DnFlPwN3ZT/2fGU/5YJlP9KIZT+9jmU/pZRlP4qaZT9soGU/TKZlPymsZT8EsmU/3LdlP7K9ZT+Ew2U/VcllPyLPZT/u1GU/ttplP3zgZT9A5mU/AexlP7/xZT9792U/Nf1lP+sCZj+gCGY/Ug5mPwEUZj+uGWY/WR9mPwElZj+mKmY/STBmP+k1Zj+IO2Y/I0FmP71GZj9TTGY/6FFmP3pXZj8JXWY/lmJmPyFoZj+qbWY/MHNmP7N4Zj80fmY/s4NmPzCJZj+qjmY/IpRmP5eZZj8Kn2Y/e6RmP+mpZj9Vr2Y/v7RmPya6Zj+Mv2Y/78RmP0/KZj+tz2Y/CdVmP2PaZj+732Y/EOVmP2PqZj+z72Y/AvVmP076Zj+Y/2Y/4ARnPyUKZz9pD2c/qhRnP+gZZz8lH2c/XyRnP5gpZz/OLmc/AjRnPzM5Zz9jPmc/kENnP7tIZz/kTWc/C1NnPzBYZz9TXWc/c2JnP5JnZz+ubGc/yHFnP+B2Zz/2e2c/CYFnPxuGZz8ri2c/OJBnP0SVZz9Nmmc/VJ9nP1mkZz9cqWc/Xa5nP1yzZz9ZuGc/VL1nP03CZz9Ex2c/OcxnPyzRZz8c1mc/C9tnP/jfZz/i5Gc/y+lnP7LuZz+X82c/efhnP1r9Zz85Amg/FgdoP/ALaD/JEGg/oBVoP3UaaD9IH2g/GSRoP+goaD+2LWg/gTJoP0o3aD8SPGg/10BoP5tFaD9dSmg/HE9oP9pTaD+WWGg/UF1oPwhiaD+/Zmg/c2toPyZwaD/XdGg/hXloPzJ+aD/dgmg/h4doPy6MaD/UkGg/d5VoPxmaaD+5nmg/WKNoP/SnaD+PrGg/KLFoP7+1aD9Uumg/575oP3nDaD8JyGg/l8xoPyPRaD+t1Wg/NtpoP73eaD9C42g/xedoP0fsaD/H8Gg/RfVoP8H5aD88/mg/tQJpPywHaT+hC2k/FRBpP4cUaT/3GGk/Zh1pP9IhaT89Jmk/pyppPw8vaT91M2k/2TdpPzs8aT+cQGk//ERpP1lJaT+1TWk/D1JpP2hWaT+/Wmk/FF9pP2djaT+5Z2k/CmxpP1hwaT+ldGk/8XhpPzp9aT+CgWk/yYVpPw6KaT9Rjmk/k5JpP9OWaT8Rm2k/Tp9pP4mjaT/Cp2k/+qtpPzGwaT9mtGk/mbhpP8u8aT/7wGk/KcVpP1bJaT+BzWk/q9FpP9PVaT/62Wk/H95pP0PiaT9l5mk/heppP6TuaT/C8mk/3vZpP/j6aT8R/2k/KANqPz4Haj9SC2o/ZQ9qP3YTaj+GF2o/lBtqP6Efaj+sI2o/tidqP74raj/FL2o/yjNqP843aj/QO2o/0T9qP9BDaj/OR2o/y0tqP8ZPaj+/U2o/uFdqP65baj+jX2o/l2NqP4lnaj96a2o/am9qP1hzaj9Ed2o/L3tqPxl/aj8Bg2o/6IZqP86Kaj+yjmo/lJJqP3WWaj9Vmmo/NJ5qPxGiaj/spWo/x6lqP5+taj93sWo/TbVqPyK5aj/1vGo/x8BqPw=="


def _dec32_table():
    import base64 as _b64
    return np.frombuffer(_b64.b64decode(_DEC32_B64), dtype=np.float32).copy()


def host_consts(cfg):
    D, S = cfg.D, cfg.S
    c = {}
    c["fconst"] = np.broadcast_to(np.array(FCONST_VALS, np.float32), (128, 8)).copy()
    dec = _dec32_table()[:D]
    beta = np.float32(1.0) - dec
    c["beta_col"] = beta.reshape(cfg.DC, 128).T.copy()
    NL = 12
    A = dec.copy()
    at = np.empty((cfg.DC, NL, 128), np.float32)
    for l in range(NL):
        at[:, l, :] = A.reshape(cfg.DC, 128)
        A = A * A
    c["Atab"] = at.reshape(cfg.DC * NL, 128).T.copy()
    c["ident32"] = np.eye(128, dtype=np.float32)
    c["ones32"] = np.ones((128, 128), dtype=np.float32)
    m = np.full((128, 4, 63), -120.0, dtype=np.float32)
    for w in range(63):
        m[2 * w:2 * w + 4, :, w] = 0.0
    c["amask"] = m.reshape(128, 252).copy()
    mt = np.full((S, 1), 2.0, dtype=np.float32)
    mt[[0, 1, S - 2, S - 1], 0] = 1.0
    c["mt_vec"] = mt
    return c


def bcast128(v):
    return np.broadcast_to(np.asarray(v, np.float32), (128, v.shape[-1])).copy()


def col128(v, DC):
    return np.asarray(v, np.float32).reshape(DC, 128).T.copy()



ENGS = ("sync", "tensor", "vector", "scalar", "gpsimd")


class Prog:
    def __init__(self, nc):
        self.nc = nc
        self.q = {e: [] for e in ENGS}
        self.cnt = {}
        self.sem_names = set()

    def tok(self, sem):
        return (sem, self.cnt.get(sem, 0))

    def emit(self, eng, fn, wait=(), inc=None, amt=1):
        w = tuple((s, v) for t in wait if t for (s, v) in [t] if v > 0)
        self.q[eng].append((w, fn, inc, amt))
        if inc is not None:
            self.sem_names.add(inc)
            self.cnt[inc] = self.cnt.get(inc, 0) + amt
            return (inc, self.cnt[inc])
        return None

    def t(self, fn, wait=(), inc="T"):
        return self.emit("tensor", fn, wait, inc)

    def v(self, fn, wait=(), inc="V"):
        w = tuple(wait) + ((self.tok("V"),) if self.cnt.get("V", 0) else ())
        return self.emit("vector", fn, w, inc)

    def a(self, fn, wait=(), inc="A"):
        w = tuple(wait) + ((self.tok("A"),) if self.cnt.get("A", 0) else ())
        return self.emit("scalar", fn, w, inc)

    def g(self, fn, wait=(), inc="G"):
        w = tuple(wait) + ((self.tok("G"),) if self.cnt.get("G", 0) else ())
        return self.emit("gpsimd", fn, w, inc)

    def dma(self, eng, stream, fn, wait=()):
        return self.emit(eng, fn, wait, "D_" + stream, 16)

    def ld(self, fn, wait=(), stream="ld"):
        return self.dma("sync", stream, fn, wait)

    def gd(self, fn, wait=(), stream="gd"):
        return self.dma("gpsimd", stream, fn, wait)

    def sd(self, fn, wait=(), stream="sd"):
        return self.dma("scalar", stream, fn, wait)

    def drain(self, eng):
        pass

    def barrier(self):
        toks = [self.tok(s) for s in ("T", "V", "A", "G")]
        toks += [self.tok(s) for s in sorted(self.sem_names) if s.startswith("D_")]
        toks = [t for t in toks if t[1] > 0]
        for e in ENGS:
            self.emit(e, lambda en: en.nop(), tuple(toks), None)

    def build(self, sems, block):
        for ename in ENGS:
            section = getattr(block, {"sync": "sync", "tensor": "tensor",
                                      "vector": "vector", "scalar": "scalar",
                                      "gpsimd": "gpsimd"}[ename])

            def mk(ename):
                def body(eng):
                    waited = {}
                    for w, fn, inc, amt in self.q[ename]:
                        for (s, v) in w:
                            if waited.get(s, 0) < v:
                                eng.wait_ge(sems[s], v)
                                waited[s] = v
                        h = fn(eng)
                        if inc is not None:
                            assert h is not None, f"inc={inc} on fn returning None"
                            h.then_inc(sems[inc], amt)
                return body

            section(mk(ename))



def mk_ln_bufs(sb, bufs, tag):
    if f"bn6_{tag}" not in bufs:
        bufs[f"bn6_{tag}"] = sb(f"lnb6_{tag}", [128, 12])
        bufs[f"bn2_{tag}"] = sb(f"lnb2_{tag}", [128, 2])
        bufs[f"rs_{tag}"] = sb(f"lnbr_{tag}", [128, 1])


def emit_layernorm_T(p, bufs, src_ap, dst_ap, n, eps, wait, tag,
                     gamma_bc=None, beta_bc=None, out2=None):
    bn6, bn2, s_rs = bufs[f"bn6_{tag}"], bufs[f"bn2_{tag}"], bufs[f"rs_{tag}"]
    nhalf = n // 2
    tv1 = p.v(lambda e: e.bn_stats(out=bn6[:, 0:6], in_=src_ap[:, 0:nhalf]), wait=wait)
    tv2 = p.v(lambda e: e.bn_stats(out=bn6[:, 6:12], in_=src_ap[:, nhalf:n]))
    tv3 = p.v(lambda e: e.bn_aggr(out=bn2[:], in_=bn6[:]))
    ta1 = p.a(lambda e: e.activation(out=s_rs[:], in_=bn2[:, 1:2], func=AF.Sqrt,
                                     bias=float(eps), scale=1.0), wait=list(wait) + [tv3])
    tv5 = p.v(lambda e: e.reciprocal(out=s_rs[:], in_=s_rs[:]), wait=[ta1])
    tv6 = p.v(lambda e: e.tensor_scalar(out=dst_ap, in0=src_ap, scalar1=bn2[:, 0:1],
                                        scalar2=s_rs[:, 0:1], op0=ALU.subtract,
                                        op1=ALU.mult))
    if gamma_bc is not None:
        p.drain("vector")
        tv7 = p.v(lambda e: e.tensor_tensor(out=dst_ap, in0=dst_ap, in1=gamma_bc,
                                            op=ALU.mult))
        p.drain("vector")
        tv8 = p.v(lambda e: e.tensor_tensor(out=dst_ap, in0=dst_ap, in1=beta_bc,
                                            op=ALU.add))
        p.drain("vector")
        return tv8
    p.drain("vector")
    return tv6


def emit_quant(p, bufs, src_ap, qs_ap, sch_ap, scl_ap, n, tag, wait, out32=None):
    nb = n // 16
    ax, sg = bufs[f"qax_{tag}"], bufs[f"qsg_{tag}"]
    u = ax
    qa, qb = bufs[f"qqa_{tag}"], bufs[f"qqb_{tag}"]
    bm, sc, rcp = bufs[f"qbm_{tag}"], bufs[f"qsc_{tag}"], bufs[f"qrcp_{tag}"]
    src3 = src_ap.rearrange("p (b s) -> p b s", s=16)

    ta1 = p.a(lambda e: e.activation(out=ax[:, 0:n], in_=src_ap, func=AF.Abs), wait=wait)
    ta2 = p.a(lambda e: e.activation(out=sg[:, 0:n], in_=src_ap, func=AF.Sign))
    p.drain("scalar")
    tv1 = p.v(lambda e: e.tensor_reduce(out=bm[:, 0:nb], in_=src3, axis=AX.X,
                                        op=ALU.max, apply_absolute_value=True),
              wait=wait)
    p.drain("vector")
    tv2 = p.v(lambda e: e.tensor_scalar(out=sc[:, 0:nb], in0=bm[:, 0:nb],
                                        scalar1=bufs["rcp6"][:, 0:1], scalar2=1e-12,
                                        op0=ALU.mult, op1=ALU.max))
    p.drain("vector")
    tv3 = p.v(lambda e: e.reciprocal(out=rcp[:, 0:nb], in_=sc[:, 0:nb]))
    p.drain("vector")
    tv7 = p.v(lambda e: e.tensor_tensor(
        out=u[:, 0:n].rearrange("p (b s) -> p b s", s=16),
        in0=ax[:, 0:n].rearrange("p (b s) -> p b s", s=16),
        in1=rcp[:, 0:nb].unsqueeze(-1).broadcast_to([128, nb, 16]), op=ALU.mult),
        wait=[ta1])
    p.drain("vector")
    p.v(lambda e: e.tensor_scalar(out=qa[:, 0:n], in0=u[:, 0:n], scalar1=0.25,
                                  scalar2=None, op0=ALU.is_gt))
    p.drain("vector")
    for th in (0.75, 1.25, 1.75):
        p.v(lambda e, th=th: e.scalar_tensor_tensor(
            out=qa[:, 0:n], in0=u[:, 0:n], scalar=th, in1=qa[:, 0:n],
            op0=ALU.is_gt, op1=ALU.add))
        p.drain("vector")
    p.v(lambda e: e.tensor_scalar(out=qb[:, 0:n], in0=u[:, 0:n], scalar1=2.5,
                                  scalar2=None, op0=ALU.is_gt))
    p.drain("vector")
    for th in (3.5, 5.0, 5.0):
        p.v(lambda e, th=th: e.scalar_tensor_tensor(
            out=qb[:, 0:n], in0=u[:, 0:n], scalar=th, in1=qb[:, 0:n],
            op0=ALU.is_gt, op1=ALU.add))
        p.drain("vector")
    p.v(lambda e: e.scalar_tensor_tensor(out=qb[:, 0:n], in0=qa[:, 0:n], scalar=0.5,
                                         in1=qb[:, 0:n], op0=ALU.mult, op1=ALU.add))
    p.drain("vector")
    p.v(lambda e: e.tensor_tensor(out=qb[:, 0:n], in0=qb[:, 0:n], in1=sg[:, 0:n],
                                  op=ALU.mult), wait=[ta2])
    p.drain("vector")
    tvq = p.v(lambda e: e.tensor_tensor(
        out=out32.rearrange("p (b s) -> p b s", s=16),
        in0=qb[:, 0:n].rearrange("p (b s) -> p b s", s=16),
        in1=sc[:, 0:nb].unsqueeze(-1).broadcast_to([128, nb, 16]), op=ALU.mult))
    p.drain("vector")
    return tvq




def build(cfg, phases=("P", "A1", "A2", "B", "C"), debug_outputs=()):
    nc = bass.Bass("TRN2", target_bir_lowering=False)
    D, I, S = cfg.D, cfg.I, cfg.S
    DC, IC = cfg.DC, cfg.IC

    X = nc.dram_tensor("x_in", [S, D], F32, kind="ExternalInput").ap()
    Wt = nc.dram_tensor("Wt", [D, D], F32, kind="ExternalInput").ap()
    Wo = nc.dram_tensor("Wo", [D, D], F32, kind="ExternalInput").ap()
    Wg = nc.dram_tensor("Wg", [2 * D, D], F32, kind="ExternalInput").ap()
    WqkvIn = nc.dram_tensor("Wqkv", [D, 3 * D], F32, kind="ExternalInput").ap()
    WoutIn = nc.dram_tensor("Wout", [D, D], F32, kind="ExternalInput").ap()
    FW = {}
    for L in (0, 1):
        FW[(L, "g")] = nc.dram_tensor(f"L{L}_wgate", [I, D], F32, kind="ExternalInput").ap()
        FW[(L, "u")] = nc.dram_tensor(f"L{L}_wup", [I, D], F32, kind="ExternalInput").ap()
        FW[(L, "d")] = nc.dram_tensor(f"L{L}_wdown", [D, I], F32, kind="ExternalInput").ap()
    CIN = {}
    for nm, shape in [("ln1_w_bc", [128, D]), ("ln1_b_bc", [128, D]),
                      ("lng_w_bc", [128, D]), ("lng_b_bc", [128, D]),
                      ("bg_bc", [128, D]), ("bout_bc", [128, D]),
                      ("lnc_w_col", [128, DC]), ("lnc_b_col", [128, DC]),
                      ("beta_col", [128, DC]), ("Atab", [128, DC * 12]),
                      ("ident32", [128, 128]), ("ones32", [128, 128]), ("amask", [128, 252]),
                      ("fconst", [128, 8])]:
        CIN[nm] = nc.dram_tensor(nm, shape, F32, kind="ExternalInput").ap()

    MT = nc.dram_tensor("mt_vec", [S, 1], F32, kind="ExternalInput").ap()
    OUT = nc.dram_tensor("out", [S, D], F32, kind="ExternalOutput").ap()

    def scr(name, shape, dt=F32):
        return nc.dram_tensor(name, shape, dt, kind="Internal").ap()

    xn_s = scr("xn_s", [S, D])
    u_s = scr("u_s", [DC, 128, S])
    cx_s = scr("cx_s", [DC, 128, S])
    a0_s = scr("a0_s", [S, D])
    x1_s = scr("x1_s", [S, D])
    xn2_s = scr("xn2_s", [S, D])
    a1_s = scr("a1_s", [S, D])
    FWp = {}
    for L in (0, 1):
        FWp[(L, "g")] = scr(f"L{L}_wgate_p", [DC, 128, I], F32)
        FWp[(L, "u")] = scr(f"L{L}_wup_p", [DC, 128, I], F32)
        FWp[(L, "d")] = scr(f"L{L}_wdown_p", [IC, 128, D], F32)

    p = Prog(nc)
    import contextlib
    stack = contextlib.ExitStack()

    def sb(name, shape, dt=F32):
        return stack.enter_context(nc.sbuf_tensor(name, shape, dt))

    def ps(name, shape, dt=F32):
        return stack.enter_context(nc.psum_tensor(name, shape, dt))

    bufs = {}

    C = {}
    for nm in CIN:
        shape = list(CIN[nm].shape)
        C[nm] = sb("c_" + nm, shape, F32)
    for nm in CIN:
        p.ld(lambda e, nm=nm: e.dma_start(out=C[nm][:], in_=CIN[nm][:]), stream="cst")
    tok_const = [p.tok("D_cst")]
    for i, val in enumerate(FCONST_VALS):
        nc.const_aps.register(F32, float(val), C["fconst"][:, i:i + 1])
    for _e in ENGS:
        p.emit(_e, lambda e: e.nop(), tuple(tok_const), None)

    bufs["rcp6"] = sb("c_rcp6", [128, 1])
    p.v(lambda e: e.memset(bufs["rcp6"][:], 6.0))
    p.drain("vector")
    p.v(lambda e: e.reciprocal(out=bufs["rcp6"][:], in_=bufs["rcp6"][:]))
    p.drain("vector")

    pb = [ps(f"pb{i}", [128, 512]) for i in range(4)]
    pw = ps("pw", [128, 1024])
    pstat = ps("pstat", [128, 512])
    ptr = ps("ptr", [128, 512])
    pb_tok = [() for _ in range(4)]
    pw_tok = [()]
    pstat_tok = [()]
    ptr_tok = [()]

    W_in = dict(Wt=Wt, Wo=Wo, Wg=Wg, Wqkv=WqkvIn, Wout=WoutIn)
    state = dict(p=p, nc=nc, cfg=cfg, sb=sb, bufs=bufs, C=C, pb=pb, pw=pw,
                 pstat=pstat, ptr=ptr, pb_tok=pb_tok, pw_tok=pw_tok,
                 pstat_tok=pstat_tok, ptr_tok=ptr_tok, tok_const=tok_const,
                 W_in=W_in, FWp=FWp)

    import contextlib as _ctx

    def scoped(fn, *args):
        p.barrier()
        with _ctx.ExitStack() as ph_stack:
            def sb_ph(name, shape, dt=F32):
                return ph_stack.enter_context(nc.sbuf_tensor(name, shape, dt))
            state["sb"] = sb_ph
            fn(state, *args)
        state["sb"] = sb

    if "P" in phases:
        scoped(emit_preprocess, FW, FWp)
    if "A1" in phases:
        scoped(emit_A1a, X, xn_s, u_s)
        scoped(emit_A1b, u_s, cx_s)
        scoped(emit_A1c, X, xn_s, cx_s, a0_s)
    if "A2" in phases:
        scoped(emit_ffn_phase, xn_s, a0_s, x1_s, 0, FWp)
    if "B" in phases:
        scoped(emit_B0, x1_s, xn2_s, u_s)
        scoped(emit_B1, u_s, cx_s)
        scoped(emit_B2, x1_s, cx_s, a1_s, MT)
    if "C" in phases:
        scoped(emit_ffn_phase, xn2_s, a1_s, OUT, 1, FWp)

    dbg_map = {"xn_s": xn_s, "a0_s": a0_s, "x1_s": x1_s, "xn2_s": xn2_s, "a1_s": a1_s}
    for nm in debug_outputs:
        src = dbg_map[nm]
        dbg = nc.dram_tensor("dbg_" + nm, [S, D], F32, kind="ExternalOutput").ap()
        tbuf = sb("dbgbuf_" + nm, [128, D])
        for r in range(S // 128):
            wdbg = [p.tok("T"), p.tok("V"), p.tok("A"), p.tok("G")]
            wdbg += [p.tok(s) for s in sorted(p.sem_names) if s.startswith("D_")]
            tk = p.ld(lambda e, r=r, src=src, tbuf=tbuf: e.dma_start(
                out=tbuf[:], in_=src[r * 128:(r + 1) * 128, :]),
                wait=wdbg, stream="dbgl")
            p.gd(lambda e, r=r, dbg=dbg, tbuf=tbuf: e.dma_start(
                out=dbg[r * 128:(r + 1) * 128, :], in_=tbuf[:]), wait=[tk],
                stream="dbgs")

    final_waits = [p.tok(s) for s in sorted(p.sem_names) if s.startswith("D_")]
    p.emit("gpsimd", lambda e: e.nop(), tuple(final_waits), None)

    sems = {}
    for s_name in sorted(p.sem_names | {"T", "V", "A", "G"}):
        sems[s_name] = stack.enter_context(nc.semaphore(s_name))
    with stack:
        with nc.Block() as block:
            p.build(sems, block)
    return nc



def emit_preprocess(st, FW, FWp):
    p, cfg, sb, bufs, C = st["p"], st["cfg"], st["sb"], st["bufs"], st["C"]
    D, I, DC, IC = cfg.D, cfg.I, cfg.DC, cfg.IC
    ptr, ptr_tok = st["ptr"], st["ptr_tok"]
    nmax = max(D, I)
    wb32 = [sb(f"pp_w32_{i}", [128, nmax]) for i in range(2)]
    wq32 = [sb(f"pp_wq32_{i}", [128, nmax]) for i in range(2)]
    tr32 = [sb(f"pp_tr32_{i}", [128, max(DC, IC), 128]) for i in range(2)]
    for par in (0, 1):
        for nm, shape, dt in [(f"qax_pp{par}", [128, nmax], F32),
                              (f"qsg_pp{par}", [128, nmax], F32),
                              (f"qqa_pp{par}", [128, nmax], F32),
                              (f"qqb_pp{par}", [128, nmax], F32),
                              (f"qbm_pp{par}", [128, nmax // 16], F32),
                              (f"qsc_pp{par}", [128, nmax // 16], F32),
                              (f"qrcp_pp{par}", [128, nmax // 16], F32)]:
            bufs[nm] = sb(nm, shape, dt)

    q_tok = [(), ()]
    st_tok = [(), ()]
    ctr = [0]
    ready = {}

    def do_ffn_weight(src, dst, nrows, kdim, name):
        nrc = nrows // 128
        kcc = kdim // 128
        for rc in range(nrc):
            par = ctr[0] % 2
            ctr[0] += 1
            w_l = []
            if q_tok[par]:
                w_l.append(q_tok[par])
            tk = p.ld(lambda e, src=src, rc=rc, par=par, kdim=kdim: e.dma_start(
                out=wb32[par][:, 0:kdim], in_=src[rc * 128:(rc + 1) * 128, 0:kdim]),
                wait=w_l, stream=f"ppl{par}")
            tq = emit_quant(p, bufs, wb32[par][:, 0:kdim], None,
                            None, None, kdim, f"pp{par}", [tk],
                            out32=wq32[par][:, 0:kdim])
            q_tok[par] = tq
            te = None
            for kc in range(kcc):
                w_tr = [tq]
                if ptr_tok[0]:
                    w_tr.append(ptr_tok[0])
                tt = p.t(lambda e, kc=kc, par=par: e.transpose(
                    ptr[0:128, 0:128], wq32[par][:, kc * 128:(kc + 1) * 128],
                    C["ident32"][:]), wait=w_tr)
                w_te = [tt]
                if kc == 0 and st_tok[par]:
                    w_te.append(st_tok[par])
                te = p.a(lambda e, kc=kc, par=par: e.copy(
                    out=tr32[par][:, kc, :], in_=ptr[0:128, 0:128]), wait=w_te)
                ptr_tok[0] = te
            tg = p.gd(lambda e, rc=rc, dst=dst, kcc=kcc, par=par: e.dma_start(
                out=dst[0:kcc, 0:128, rc * 128:(rc + 1) * 128].transpose([1, 0, 2]),
                in_=tr32[par][:, 0:kcc, :]),
                wait=[te], stream=f"pps{par}")
            st_tok[par] = tg
        ready[name] = (p.tok("D_pps0"), p.tok("D_pps1"))

    do_ffn_weight(FW[(0, "g")], FWp[(0, "g")], I, D, "L0g")
    do_ffn_weight(FW[(0, "u")], FWp[(0, "u")], I, D, "L0u")
    do_ffn_weight(FW[(0, "d")], FWp[(0, "d")], D, I, "L0d")
    do_ffn_weight(FW[(1, "g")], FWp[(1, "g")], I, D, "L1g")
    do_ffn_weight(FW[(1, "u")], FWp[(1, "u")], I, D, "L1u")
    do_ffn_weight(FW[(1, "d")], FWp[(1, "d")], D, I, "L1d")
    st["pp_done"] = [p.tok("D_pps0"), p.tok("D_pps1"), p.tok("V"), p.tok("A"), p.tok("T")]



def emit_A1a(st, X, xn_s, u_s):
    p, cfg, sb, bufs, C = st["p"], st["cfg"], st["sb"], st["bufs"], st["C"]
    D, DC, TA, S = cfg.D, cfg.DC, cfg.TA, cfg.S
    TC = TA // 128
    NT = S // TA
    pb, ptr = st["pb"], st["ptr"]
    pb_tok, ptr_tok = st["pb_tok"], st["ptr_tok"]
    W_in = st["W_in"]

    x_t = sb("a1_x", [128, TC, D])
    xn_t = sb("a1_xn", [128, TC, D])
    lx_t = sb("a1_lx", [128, TC, D])
    xf = sb("a1_xf", [128, DC, TA])
    u_sb = [sb(f"a1_u{i}", [128, TA]) for i in range(2)]
    for c in range(TC):
        mk_ln_bufs(sb, bufs, f"a1{c}")
    wt_buf = [sb(f"a1_wt{i}", [128, DC, 128]) for i in range(2)]
    wt_tok = [(), ()]
    us_tok = [(), ()]

    tile_done = [st.get("pp_done", [])[:] or []]

    for j in range(NT):
        t0 = j * TA
        tk_x = p.ld(lambda e, t0=t0: e.dma_start(
            out=x_t[:],
            in_=X[t0:t0 + TA, :].rearrange("(c p) d -> p c d", p=128)),
            wait=tile_done[0], stream="a1x")
        t_bln = [emit_layernorm_T(p, bufs, x_t[:, c, :],
                                  xn_t[:, c, :], D, cfg.eps_blk,
                                  [tk_x], f"a1{c}") for c in range(TC)]
        tk_xns = p.gd(lambda e, t0=t0: e.dma_start(
            out=xn_s[t0:t0 + TA, :].rearrange("(c p) d -> p c d", p=128),
            in_=xn_t[:]), wait=t_bln, stream="a1xn")
        t_ln1 = [emit_layernorm_T(p, bufs, xn_t[:, c, :], lx_t[:, c, :], D,
                                  cfg.eps_l0, [t_bln[c]], f"a1{c}",
                                  gamma_bc=C["ln1_w_bc"][:], beta_bc=C["ln1_b_bc"][:])
                 for c in range(TC)]
        tv_lx = t_ln1[-1]
        for c in range(TC):
            for dc in range(DC):
                w = [t_ln1[c]]
                if ptr_tok[0]:
                    w.append(ptr_tok[0])
                tt = p.t(lambda e, c=c, dc=dc: e.transpose(
                    ptr[0:128, 0:128], lx_t[:, c, dc * 128:(dc + 1) * 128],
                    C["ident32"][:]), wait=w)
                ta_h = p.a(lambda e, c=c, dc=dc: e.copy(
                    out=xf[:, dc, c * 128:(c + 1) * 128], in_=ptr[0:128, 0:128]),
                    wait=[tt])
                ptr_tok[0] = ta_h
        tv_xf = ta_h
        for hc in range(DC):
            par = hc % 2
            w_w = [wt_tok[par]] if wt_tok[par] else []
            tk_w = p.ld(lambda e, hc=hc, par=par: e.dma_start(
                out=wt_buf[par][:],
                in_=W_in["Wt"][:, hc * 128:(hc + 1) * 128].rearrange(
                    "(c p) h -> p c h", p=128)),
                wait=w_w, stream=f"a1wt{par}")
            bank = pb[par]
            w0 = [tk_w, tv_xf]
            if pb_tok[par]:
                w0.append(pb_tok[par])
            mm = None
            for dc in range(DC):
                first = (dc == 0)
                last = (dc == DC - 1)
                mm = p.t(lambda e, hc=hc, par=par, dc=dc,
                         first=first, last=last, bank=bank: e.matmul(
                    bank[:, 0:TA], wt_buf[par][:, dc, :], xf[:, dc, :],
                    start=first, stop=last), wait=w0 if first else ())
            wt_tok[par] = mm
            w_u = [mm]
            if us_tok[par]:
                w_u.append(us_tok[par])
            tv_u = p.v(lambda e, hc=hc, par=par, bank=bank: e.tensor_scalar(
                out=u_sb[par][:], in0=bank[:, 0:TA],
                scalar1=C["beta_col"][:, hc:hc + 1],
                scalar2=None, op0=ALU.mult), wait=w_u)
            pb_tok[par] = tv_u
            p.drain("vector")
            tk_us = p.gd(lambda e, hc=hc, par=par, t0=t0: e.dma_start(
                out=u_s[hc, :, t0:t0 + TA], in_=u_sb[par][:]),
                wait=[tv_u], stream=f"a1us{par}")
            us_tok[par] = tk_us
        tile_done[0] = [tk_xns, p.tok("T"), p.tok("A"), p.tok("V")]
    st["prev_done"] = [p.tok("D_a1xn"), p.tok("D_a1us0"), p.tok("D_a1us1")]


def _sl2(buf, col0, n):
    base = buf[:, col0:col0 + 1]
    return bass.AP(base.tensor, base.offset, [list(base.ap[0]), [2, n]])


def emit_A1b(st, u_s, cx_s):
    p, cfg, sb, bufs, C = st["p"], st["cfg"], st["sb"], st["bufs"], st["C"]
    DC, S = cfg.DC, cfg.S
    NL = S.bit_length() - 1
    ub = [sb(f"a1b_ub{i}", [128, 2 * S]) for i in range(2)]
    off = [0]
    for l in range(1, NL + 1):
        off.append(off[-1] + (S >> (l - 1)))
    ld_tok = [(), ()]
    stv_tok = [(), ()]
    st_tok = [(), ()]
    prev_d = st.get("prev_done", [])

    for hc in range(DC):
        par = hc % 2
        b = ub[par]
        w_l = list(prev_d) if hc < 2 else []
        if st_tok[par]:
            w_l.append(st_tok[par])
        if stv_tok[par]:
            w_l.append(stv_tok[par])
        tk = p.ld(lambda e, hc=hc, par=par, b=b: e.dma_start(
            out=b[:, 0:S], in_=u_s[hc, :, :]), wait=w_l, stream=f"a1b{par}")
        ld_tok[par] = tk
        tv = None
        for l in range(1, NL + 1):
            nl = S >> l
            w = [tk] if l == 1 else []
            tv = p.v(lambda e, l=l, nl=nl, hc=hc, b=b: e.scalar_tensor_tensor(
                out=b[:, off[l]:off[l] + nl],
                in0=_sl2(b, off[l - 1], nl),
                scalar=C["Atab"][:, hc * 12 + (l - 1):hc * 12 + l],
                in1=_sl2(b, off[l - 1] + 1, nl),
                op0=ALU.mult, op1=ALU.add), wait=w)
            p.drain("vector")
        for l in range(NL, 0, -1):
            nl = S >> l
            if nl > 1:
                tv = p.v(lambda e, l=l, nl=nl, hc=hc, b=b: e.scalar_tensor_tensor(
                    out=_sl2(b, off[l - 1] + 2, nl - 1),
                    in0=b[:, off[l]:off[l] + nl - 1],
                    scalar=C["Atab"][:, hc * 12 + (l - 1):hc * 12 + l],
                    in1=_sl2(b, off[l - 1] + 2, nl - 1),
                    op0=ALU.mult, op1=ALU.add))
                p.drain("vector")
            tv = p.v(lambda e, l=l, nl=nl, b=b: e.tensor_copy(
                out=_sl2(b, off[l - 1] + 1, nl),
                in_=b[:, off[l]:off[l] + nl]))
            p.drain("vector")
        stv_tok[par] = tv
        tg = p.gd(lambda e, hc=hc, par=par, b=b: e.dma_start(
            out=cx_s[hc, :, :], in_=b[:, 0:S]), wait=[tv], stream=f"a1bs{par}")
        st_tok[par] = tg
    st["prev_done"] = [p.tok("D_a1bs0"), p.tok("D_a1bs1")]


def emit_A1c(st, X, xn_s, cx_s, a0_s):
    p, cfg, sb, bufs, C = st["p"], st["cfg"], st["sb"], st["bufs"], st["C"]
    D, DC, TA, S = cfg.D, cfg.DC, cfg.TA, cfg.S
    TC = TA // 128
    NT = S // TA
    pb, pw, pstat, ptr = st["pb"], st["pw"], st["pstat"], st["ptr"]
    pb_tok, pw_tok, pstat_tok, ptr_tok = st["pb_tok"], st["pw_tok"], st["pstat_tok"], st["ptr_tok"]
    W_in = st["W_in"]

    x_t = sb("a1c_x", [128, TC, D])
    xn_t = sb("a1c_xn", [128, TC, D])
    lx_t = sb("a1c_lx", [128, TC, D])
    xf = sb("a1c_xf", [128, DC, TA])
    cx = sb("a1c_cx", [128, DC, TA])
    lncx = sb("a1c_lncx", [128, DC, TA])
    af = sb("a1c_af", [128, DC, TA])
    at_t = xn_t
    g_t = sb("a1c_g", [128, TC, D])
    sq_sb = sb("a1c_sq", [128, TA])
    mu_b = sb("a1c_mu", [128, TA])
    ms_b = sb("a1c_ms", [128, TA])
    var_b = sb("a1c_var", [128, TA])
    rs_b = sb("a1c_rs", [128, TA])
    rsg_b = sb("a1c_rsg", [128, TA])
    tmp1 = sb("a1c_tmp1", [128, TA])
    for c in range(TC):
        mk_ln_bufs(sb, bufs, f"c{c}")
        mk_ln_bufs(sb, bufs, f"g{c}")
    wo_buf = [sb(f"a1c_wo{i}", [128, DC, 128]) for i in range(2)]
    wg_buf = [sb(f"a1c_wg{i}", [128, D]) for i in range(2)]
    wo_tok = [(), ()]
    wg_tok = [(), ()]

    prev_d = st.get("prev_done", [])
    tile_done = [list(prev_d)]

    for j in range(NT):
        t0 = j * TA
        tk_x = p.ld(lambda e, t0=t0: e.dma_start(
            out=x_t[:],
            in_=X[t0:t0 + TA, :].rearrange("(c p) d -> p c d", p=128)),
            wait=tile_done[0], stream="a1cx")
        tk_xn = p.ld(lambda e, t0=t0: e.dma_start(
            out=xn_t[:],
            in_=xn_s[t0:t0 + TA, :].rearrange("(c p) d -> p c d", p=128)),
            wait=tile_done[0], stream="a1cxn")
        tk_cx = p.ld(lambda e, t0=t0: e.dma_start(
            out=cx[:],
            in_=cx_s[0:DC, 0:128, t0:t0 + TA].transpose([1, 0, 2])),
            wait=tile_done[0], stream="a1ccx")
        t_ln1 = [emit_layernorm_T(p, bufs, xn_t[:, c, :], lx_t[:, c, :], D,
                                  cfg.eps_l0, [tk_xn], f"c{c}",
                                  gamma_bc=C["ln1_w_bc"][:], beta_bc=C["ln1_b_bc"][:])
                 for c in range(TC)]
        tv_lx = t_ln1[-1]
        for c in range(TC):
            for dc in range(DC):
                w = [t_ln1[c]]
                if ptr_tok[0]:
                    w.append(ptr_tok[0])
                tt = p.t(lambda e, c=c, dc=dc: e.transpose(
                    ptr[0:128, 0:128], lx_t[:, c, dc * 128:(dc + 1) * 128],
                    C["ident32"][:]), wait=w)
                ta_h = p.a(lambda e, c=c, dc=dc: e.copy(
                    out=xf[:, dc, c * 128:(c + 1) * 128], in_=ptr[0:128, 0:128]),
                    wait=[tt])
                ptr_tok[0] = ta_h
        tv_xf = ta_h
        last_sq_mm = None
        mm1 = None
        for hc in range(DC):
            w = [tk_cx]
            if hc == 0 and pstat_tok[0]:
                w.append(pstat_tok[0])
            if hc == 0 and pb_tok[2]:
                w.append(pb_tok[2])
            mm1 = p.t(lambda e, hc=hc, first=(hc == 0): e.matmul(
                pstat[:, 0:TA], C["ones32"][:],
                cx[:, hc, :], start=first, stop=(hc == DC - 1)),
                wait=w)
            ta_sq = p.a(lambda e, hc=hc: e.activation(
                out=sq_sb[:], in_=cx[:, hc, :], func=AF.Square),
                wait=[tk_cx] + ([last_sq_mm] if last_sq_mm else []))
            mm2 = p.t(lambda e, hc=hc, first=(hc == 0): e.matmul(
                pb[2][:, 0:TA], C["ones32"][:],
                sq_sb[:], start=first, stop=(hc == DC - 1)),
                wait=[ta_sq])
            last_sq_mm = mm2
        tv_mu = p.v(lambda e: e.tensor_scalar(
            out=mu_b[:], in0=pstat[:, 0:TA], scalar1=1.0 / D, scalar2=None,
            op0=ALU.mult), wait=[mm1, last_sq_mm])
        pstat_tok[0] = tv_mu
        tv_ms = p.v(lambda e: e.tensor_scalar(
            out=ms_b[:], in0=pb[2][:, 0:TA], scalar1=1.0 / D, scalar2=None,
            op0=ALU.mult))
        pb_tok[2] = tv_ms
        p.drain("vector")
        p.v(lambda e: e.tensor_tensor(out=var_b[:], in0=mu_b[:], in1=mu_b[:],
                                      op=ALU.mult))
        p.drain("vector")
        tv_var = p.v(lambda e: e.tensor_tensor(out=var_b[:], in0=ms_b[:], in1=var_b[:],
                                               op=ALU.subtract))
        p.drain("vector")
        ta_sd = p.a(lambda e: e.activation(out=rs_b[:], in_=var_b[:], func=AF.Sqrt,
                                           bias=float(cfg.eps_l0), scale=1.0),
                    wait=[tv_var])
        tv_rs = p.v(lambda e: e.reciprocal(out=rs_b[:], in_=rs_b[:]), wait=[ta_sd])
        p.drain("vector")
        tv_lnc = []
        for hc in range(DC):
            p.v(lambda e, hc=hc: e.tensor_scalar(
                out=rsg_b[:], in0=rs_b[:], scalar1=C["lnc_w_col"][:, hc:hc + 1],
                scalar2=None, op0=ALU.mult))
            p.drain("vector")
            p.v(lambda e, hc=hc: e.tensor_tensor(
                out=tmp1[:], in0=cx[:, hc, :], in1=mu_b[:], op=ALU.subtract))
            p.drain("vector")
            p.v(lambda e, hc=hc: e.tensor_tensor(
                out=tmp1[:], in0=tmp1[:], in1=rsg_b[:], op=ALU.mult))
            p.drain("vector")
            tv_t3 = p.v(lambda e, hc=hc: e.tensor_scalar(
                out=lncx[:, hc, :], in0=tmp1[:], scalar1=C["lnc_b_col"][:, hc:hc + 1],
                scalar2=None, op0=ALU.add))
            p.drain("vector")
            tv_lnc.append(tv_t3)
        ta_af = [None] * DC
        for dco in range(DC):
            par = dco % 2
            w_w = [wo_tok[par]] if wo_tok[par] else []
            tk_w = p.ld(lambda e, dco=dco, par=par: e.dma_start(
                out=wo_buf[par][:],
                in_=W_in["Wo"][:, dco * 128:(dco + 1) * 128].rearrange(
                    "(c p) d -> p c d", p=128)),
                wait=w_w, stream=f"a1cwo{par}")
            bank = pb[par]
            w0 = [tk_w, tv_lnc[-1]]
            if pb_tok[par]:
                w0.append(pb_tok[par])
            mm = None
            for hc in range(DC):
                first = (hc == 0)
                last = (hc == DC - 1)
                mm = p.t(lambda e, dco=dco, par=par, hc=hc,
                         first=first, last=last, bank=bank: e.matmul(
                    bank[:, 0:TA], wo_buf[par][:, hc, :], lncx[:, hc, :],
                    start=first, stop=last), wait=w0 if first else ())
            wo_tok[par] = mm
            ta_h = p.a(lambda e, dco=dco, bank=bank: e.copy(
                out=af[:, dco, :], in_=bank[:, 0:TA]), wait=[mm])
            pb_tok[par] = ta_h
            ta_af[dco] = ta_h
        tv_af_all = ta_h
        tv_blend = []
        tv_gp_c = [None] * TC
        for cp in range((TC + 1) // 2):
            grp = [2 * cp] + ([2 * cp + 1] if 2 * cp + 1 < TC else [])
            c0 = grp[0]
            c1 = grp[1] if len(grp) > 1 else grp[0]
            mm = None
            DH2 = min(512, D)
            NHALF = D // DH2
            halves = {}
            for h in range(NHALF):
                halves[(c0, h)] = pb[h][:, 0:DH2]
                if len(grp) > 1:
                    halves[(c1, h)] = pw[:, h * DH2:(h + 1) * DH2]
            for kc in range(2 * DC):
                parw = kc % 2
                w_w = [wg_tok[parw]] if wg_tok[parw] else []
                tk_w = p.ld(lambda e, kc=kc, parw=parw: e.dma_start(
                    out=wg_buf[parw][:],
                    in_=W_in["Wg"][kc * 128:(kc + 1) * 128, :]),
                    wait=w_w, stream=f"a1cwg{parw}")
                stat_src = xf if kc < DC else af
                scc = kc if kc < DC else kc - DC
                w0 = [tk_w, tv_af_all, tv_xf]
                if kc == 0:
                    for g in (pb_tok[0], pb_tok[1], pw_tok[0]):
                        if g:
                            w0.append(g)
                kc_first = True
                for cc in grp:
                    for half in range(NHALF):
                        first = (kc == 0)
                        last = (kc == 2 * DC - 1)
                        st_ap = stat_src[:, scc, cc * 128:(cc + 1) * 128]
                        out_ap = halves[(cc, half)]
                        w_mm = (w0 if kc == 0 else [tk_w]) if kc_first else ()
                        kc_first = False
                        mm = p.t(lambda e, st_ap=st_ap, out_ap=out_ap, parw=parw,
                                 half=half, first=first, last=last,
                                 DH2=DH2:
                                 e.matmul(out_ap, st_ap,
                                          wg_buf[parw][:, half * DH2:(half + 1) * DH2],
                                          start=first, stop=last),
                                 wait=w_mm)
                wg_tok[parw] = mm
            for cc in grp:
                for h in range(NHALF):
                    tv_gp = p.v(lambda e, cc=cc, h=h, halves=halves, DH2=DH2:
                                e.scalar_tensor_tensor(
                        out=g_t[:, cc, h * DH2:(h + 1) * DH2], in0=halves[(cc, h)],
                        scalar=1.0, in1=C["bg_bc"][:, h * DH2:(h + 1) * DH2],
                        op0=ALU.mult, op1=ALU.add),
                        wait=[mm] if (cc == c0 and h == 0) else ())
                tv_gp_c[cc] = tv_gp
            for h in range(NHALF):
                pb_tok[h] = tv_gp_c[c0]
            if len(grp) > 1:
                pw_tok[0] = tv_gp_c[c1]
        for dco in range(DC):
            for c in range(TC):
                w_tr = [ta_af[dco]]
                if ptr_tok[0]:
                    w_tr.append(ptr_tok[0])
                tt2 = p.t(lambda e, dco=dco, c=c: e.transpose(
                    ptr[0:128, 0:128], af[:, dco, c * 128:(c + 1) * 128],
                    C["ident32"][:]), wait=w_tr)
                w_at = [tt2]
                if dco == 0 and c == 0:
                    w_at.append(tv_lx)
                tv_at = p.v(lambda e, dco=dco, c=c: e.tensor_copy(
                    out=at_t[:, c, dco * 128:(dco + 1) * 128], in_=ptr[0:128, 0:128]),
                    wait=w_at)
                ptr_tok[0] = tv_at
        tv_at_all = tv_at
        for c in range(TC):
            tki = emit_layernorm_T(p, bufs, g_t[:, c, :], g_t[:, c, :], D,
                                   cfg.eps_l0, [tv_gp_c[c]], f"g{c}",
                                   gamma_bc=C["lng_w_bc"][:], beta_bc=C["lng_b_bc"][:])
            ta_g = p.a(lambda e, c=c: e.activation(
                out=g_t[:, c, :], in_=g_t[:, c, :], func=AF.Sigmoid), wait=[tki])
            p.drain("scalar")
            tv_b1 = p.v(lambda e, c=c: e.tensor_tensor(
                out=at_t[:, c, :], in0=at_t[:, c, :], in1=lx_t[:, c, :],
                op=ALU.subtract), wait=[ta_g, tv_at_all])
            p.drain("vector")
            tv_b2 = p.v(lambda e, c=c: e.tensor_tensor(
                out=at_t[:, c, :], in0=g_t[:, c, :], in1=at_t[:, c, :], op=ALU.mult))
            p.drain("vector")
            tv_b3 = p.v(lambda e, c=c: e.tensor_tensor(
                out=at_t[:, c, :], in0=at_t[:, c, :], in1=lx_t[:, c, :], op=ALU.add))
            p.drain("vector")
            tv_b4 = p.v(lambda e, c=c: e.tensor_tensor(
                out=at_t[:, c, :], in0=at_t[:, c, :], in1=x_t[:, c, :], op=ALU.add),
                wait=[tk_x])
            p.drain("vector")
            tv_blend.append(tv_b4)
        tk_a0 = p.gd(lambda e, t0=t0: e.dma_start(
            out=a0_s[t0:t0 + TA, :].rearrange("(c p) d -> p c d", p=128),
            in_=at_t[:]), wait=tv_blend, stream="a1a0")
        tile_done[0] = [tv_blend[-1], tk_a0, p.tok("T"), p.tok("A"), p.tok("V")]
    st["prev_done"] = [p.tok("D_a1xn"), p.tok("D_a1a0")]



def emit_ffn_phase(st, xn_src, attn_src, dst, L, FWp):
    p, cfg, sb, bufs, C = st["p"], st["cfg"], st["sb"], st["bufs"], st["C"]
    D, I, DC, IC, TF, S = cfg.D, cfg.I, cfg.DC, cfg.IC, cfg.TF, cfg.S
    TC = TF // 128
    NT = S // TF
    NQ = (I + 511) // 512
    pb, pw, ptr = st["pb"], st["pw"], st["ptr"]
    pb_tok, pw_tok, ptr_tok = st["pb_tok"], st["pw_tok"], st["ptr_tok"]
    ready = st.get("pp_ready", {})
    pfx = f"f{L}"

    A = dict(
        xn_t=sb(pfx + "_xn", [128, TC, D]),
        xq=sb(pfx + "_xq", [128, D]),
        xqF=sb(pfx + "_xqF", [128, DC, TF]),
        sg_sb=[sb(f"{pfx}_sg{i}", [128, 512]) for i in range(2)],
        h_nq=[sb(f"{pfx}_h{i}", [128, 512]) for i in range(2)],
        hq=[sb(f"{pfx}_hq{i}", [128, 512]) for i in range(2)],
        hqT=[sb(f"{pfx}_hqT{i}", [128, 4, 128]) for i in range(2)],
        wg_b=[sb(f"{pfx}_wg{i}", [128, DC, 512]) for i in range(2)],
        wu_b=[sb(f"{pfx}_wu{i}", [128, DC, 512]) for i in range(2)],
        wd_b=[sb(f"{pfx}_wd{i}", [128, 4, D]) for i in range(2)],
        a0_sl=[sb(f"{pfx}_a00", [128, D])],
        out_sl=[sb(f"{pfx}_out0", [128, D])],
    )
    for tag, n in ((pfx + "x", D), (pfx + "h", 512)):
        for nm, shape, dt in [
                (f"qax_{tag}", [128, n], F32), (f"qsg_{tag}", [128, n], F32),
                (f"qqa_{tag}", [128, n], F32),
                (f"qqb_{tag}", [128, n], F32), (f"qbm_{tag}", [128, n // 16], F32),
                (f"qsc_{tag}", [128, n // 16], F32), (f"qrcp_{tag}", [128, n // 16], F32)]:
            bufs[nm] = sb(nm, shape, dt)
    xn_t = A["xn_t"]
    acc = xn_t

    wgu_tok = [(), ()]
    wd_tok = [(), ()]
    wld_tok = [None, None]
    a0_tok = [(), ()]
    out_tok = [(), ()]
    prev_d = st.get("prev_done", [])
    tile_done = [st.get("pp_done", [])[:] + prev_d]

    gkey, ukey, dkey = (L, "g"), (L, "u"), (L, "d")
    rg = [ready[k] for k in (f"L{L}g",) if k in ready]
    for j in range(NT):
        t0 = j * TF
        tk_xn = p.ld(lambda e, t0=t0: e.dma_start(
            out=xn_t[:], in_=xn_src[t0:t0 + TF, :].rearrange("(c p) d -> p c d", p=128)),
            wait=tile_done[0] + rg, stream=pfx + "xn")
        xq_state = {"tprev": tk_xn, "tt_last": None}
        tv_xqf_tc = [None] * TC

        def emit_xq(tc):
            tq = emit_quant(p, bufs, xn_t[:, tc, :], None, None, None,
                            D, pfx + "x",
                            [tk_xn, xq_state["tprev"]] +
                            ([xq_state["tt_last"]] if xq_state["tt_last"] else []),
                            out32=A["xq"][:, 0:D])
            te = None
            for dc in range(DC):
                w_tr = [tq]
                if ptr_tok[0]:
                    w_tr.append(ptr_tok[0])
                tt = p.t(lambda e, dc=dc: e.transpose(
                    ptr[0:128, 0:128], A["xq"][:, dc * 128:(dc + 1) * 128],
                    C["ident32"][:]), wait=w_tr)
                xq_state["tt_last"] = tt
                te = p.a(lambda e, dc=dc, tc=tc: e.copy(
                    out=A["xqF"][:, dc, tc * 128:(tc + 1) * 128], in_=ptr[0:128, 0:128]),
                    wait=[tt])
                ptr_tok[0] = te
            xq_state["tprev"] = te
            tv_xqf_tc[tc] = te

        emit_xq(0)
        NIT = NQ * TC
        gu_done = [None] * NIT
        h_done = [None] * NIT
        q_done = [None] * NIT
        tr_done = [None] * NIT
        dn_done = [None] * NIT
        ev_done = [None] * NIT
        tkd_of = [None] * NQ

        def emit_gu(it):
            nq, tc = divmod(it, TC)
            wq = min(512, I - nq * 512)
            par = nq % 2
            if tc == 0:
                w_w = [wgu_tok[par]] if wgu_tok[par] else []
                tk_g = p.ld(lambda e, nq=nq, par=par, wq=wq: e.dma_start(
                    out=A["wg_b"][par][:, :, 0:wq],
                    in_=FWp[gkey][0:DC, 0:128, nq * 512:nq * 512 + wq].transpose([1, 0, 2])),
                    wait=w_w, stream=pfx + f"wg{par}")
                tk_u = p.ld(lambda e, nq=nq, par=par, wq=wq: e.dma_start(
                    out=A["wu_b"][par][:, :, 0:wq],
                    in_=FWp[ukey][0:DC, 0:128, nq * 512:nq * 512 + wq].transpose([1, 0, 2])),
                    wait=[], stream=pfx + f"wu{par}")
                w_w = [wd_tok[par]] if wd_tok[par] else []
                NS = wq // 128
                tk_d = p.ld(lambda e, nq=nq, par=par, NS=NS: e.dma_start(
                    out=A["wd_b"][par][:, 0:NS, :],
                    in_=FWp[dkey][nq * 4:nq * 4 + NS, 0:128, :].transpose([1, 0, 2])),
                    wait=w_w, stream=pfx + f"wd{par}")
                wld_tok[par] = (tk_g, tk_u)
                tkd_of[nq] = tk_d
            tk_g, tk_u = wld_tok[par]
            bp = it % 2
            gb = pb[2 * bp]
            ub = pb[2 * bp + 1]
            w0 = [tk_g, tk_u, tv_xqf_tc[tc]]
            for gg in (pb_tok[2 * bp], pb_tok[2 * bp + 1]):
                if gg:
                    w0.append(gg)
            mmg = mmu = None
            for kc in range(DC):
                stat = A["xqF"][:, kc, tc * 128:(tc + 1) * 128]
                fg = (kc == 0)
                lg = (kc == DC - 1)
                mmg = p.t(lambda e, stat=stat, kc=kc, par=par,
                          wq=wq, gb=gb, fg=fg, lg=lg: e.matmul(
                    gb[:, 0:wq], stat, A["wg_b"][par][:, kc, 0:wq],
                    start=fg, stop=lg), wait=w0 if fg else ())
                mmu = p.t(lambda e, stat=stat, kc=kc, par=par,
                          wq=wq, ub=ub, fg=fg, lg=lg: e.matmul(
                    ub[:, 0:wq], stat, A["wu_b"][par][:, kc, 0:wq],
                    start=fg, stop=lg))
            wgu_tok[par] = mmu
            gu_done[it] = mmu
            return gb, ub

        def emit_quant_h(it, gb, ub):
            nq, tc = divmod(it, TC)
            wq = min(512, I - nq * 512)
            bp = it % 2
            w_sg = [gu_done[it]]
            if h_done[it - 1] if it else None:
                w_sg.append(h_done[it - 1])
            ta_sg = p.a(lambda e, gb=gb, wq=wq, bp=bp: e.activation(
                out=A["sg_sb"][bp][:, 0:wq], in_=gb[:, 0:wq], func=AF.Silu),
                wait=w_sg)
            tv_h = p.v(lambda e, ub=ub, wq=wq, bp=bp: e.tensor_tensor(
                out=A["h_nq"][bp][:, 0:wq], in0=A["sg_sb"][bp][:, 0:wq],
                in1=ub[:, 0:wq], op=ALU.mult), wait=[ta_sg])
            pb_tok[2 * bp] = tv_h
            pb_tok[2 * bp + 1] = tv_h
            h_done[it] = tv_h
            w_q = [tv_h]
            if it >= 2 and tr_done[it - 2]:
                w_q.append(tr_done[it - 2])
            tqh = emit_quant(p, bufs, A["h_nq"][bp][:, 0:wq], None, None, None,
                             wq, pfx + "h", w_q, out32=A["hq"][bp][:, 0:wq])
            q_done[it] = tqh

        def emit_tr_dn_ev(it):
            nq, tc = divmod(it, TC)
            wq = min(512, I - nq * 512)
            NS = wq // 128
            par = nq % 2
            bp = it % 2
            te = None
            for ns in range(NS):
                w_tr = [q_done[it]]
                if ptr_tok[0]:
                    w_tr.append(ptr_tok[0])
                if ns == 0 and it >= 2 and dn_done[it - 2]:
                    w_tr.append(dn_done[it - 2])
                tt = p.t(lambda e, ns=ns, bp=bp: e.transpose(
                    ptr[0:128, 0:128], A["hq"][bp][:, ns * 128:(ns + 1) * 128],
                    C["ident32"][:]), wait=w_tr)
                te = p.a(lambda e, ns=ns, bp=bp: e.copy(
                    out=A["hqT"][bp][:, ns, :], in_=ptr[0:128, 0:128]), wait=[tt])
                ptr_tok[0] = te
            tr_done[it] = te
            w0d = [tkd_of[nq], te]
            if it and ev_done[it - 1]:
                w0d.append(ev_done[it - 1])
            mmd = None
            DH2 = min(512, D)
            NH2 = D // DH2
            for ns in range(NS):
                fg = (ns == 0)
                lg = (ns == NS - 1)
                for half in range(NH2):
                    mmd = p.t(lambda e, ns=ns, par=par, bp=bp,
                              half=half, fg=fg, lg=lg, DH2=DH2: e.matmul(
                        pw[:, half * DH2:(half + 1) * DH2], A["hqT"][bp][:, ns, :],
                        A["wd_b"][par][:, ns, half * DH2:(half + 1) * DH2],
                        start=fg, stop=lg),
                        wait=w0d if (fg and half == 0) else ())
            wd_tok[par] = mmd
            dn_done[it] = mmd
            if nq == 0:
                ev = p.v(lambda e, tc=tc: e.tensor_copy(
                    out=acc[:, tc, :], in_=pw[:, 0:D]), wait=[mmd])
            else:
                ev = p.v(lambda e, tc=tc: e.tensor_tensor(
                    out=acc[:, tc, :], in0=acc[:, tc, :], in1=pw[:, 0:D],
                    op=ALU.add), wait=[mmd] + ([ev_tok[tc][nq - 1]] if ev_tok[tc][nq - 1] else []))
            pw_tok[0] = ev
            ev_done[it] = ev
            ev_tok[tc][nq] = ev

        ev_tok = [[None] * NQ for _ in range(TC)]
        gu_banks = [None] * NIT
        gu_banks[0] = emit_gu(0)
        if TC > 1:
            emit_xq(1)
        for it in range(NIT):
            emit_quant_h(it, *gu_banks[it])
            if it + 2 < TC:
                emit_xq(it + 2)
            if it + 1 < NIT:
                gu_banks[it + 1] = emit_gu(it + 1)
            emit_tr_dn_ev(it)
        st_toks = []
        for tc in range(TC):
            par = 0
            w_a = [a0_tok[par]] if a0_tok[par] else []
            if j == 0 and tc < 2:
                w_a = w_a + prev_d
            tk_a = p.ld(lambda e, t0=t0, tc=tc: e.dma_start(
                out=A["a0_sl"][0][:],
                in_=attn_src[t0 + tc * 128:t0 + (tc + 1) * 128, :]),
                wait=w_a, stream=pfx + f"a0{par}")
            w_o = [tk_a, ev_tok[tc][NQ - 1]]
            if out_tok[par]:
                w_o.append(out_tok[par])
            tv_o = p.v(lambda e, tc=tc, par=par: e.tensor_tensor(
                out=A["out_sl"][par][:], in0=acc[:, tc, :], in1=A["a0_sl"][par][:],
                op=ALU.add), wait=w_o)
            p.drain("vector")
            tk_o = p.gd(lambda e, t0=t0, tc=tc, par=par: e.dma_start(
                out=dst[t0 + tc * 128:t0 + (tc + 1) * 128, :],
                in_=A["out_sl"][par][:]), wait=[tv_o], stream=pfx + f"o{par}")
            a0_tok[par] = tv_o
            out_tok[par] = tk_o
            st_toks.append(tk_o)
        tile_done[0] = st_toks + [p.tok("V"), p.tok("A"), p.tok("T")]
    st["prev_done"] = [p.tok(f"D_{pfx}o0")]



def emit_B0(st, x1_s, xn2_s, xnF_s):
    p, cfg, sb, bufs, C = st["p"], st["cfg"], st["sb"], st["bufs"], st["C"]
    D, DC, S = cfg.D, cfg.DC, cfg.S
    TA = 512
    TC = TA // 128
    NT = S // TA
    ptr = st["ptr"]
    ptr_tok = st["ptr_tok"]
    x1_t = sb("b0_x1", [128, TC, D])
    xn2_t = sb("b0_xn2", [128, TC, D])
    xnF = sb("b0_xnF", [128, DC, TA])
    for c in range(TC):
        mk_ln_bufs(sb, bufs, f"b0{c}")
    tile_done = [list(st.get("prev_done", []))]
    for j in range(NT):
        t0 = j * TA
        tk_x = p.ld(lambda e, t0=t0: e.dma_start(
            out=x1_t[:],
            in_=x1_s[t0:t0 + TA, :].rearrange("(c p) d -> p c d", p=128)),
            wait=tile_done[0], stream="b0x")
        t_bln = [emit_layernorm_T(p, bufs, x1_t[:, c, :], xn2_t[:, c, :], D,
                                  cfg.eps_blk, [tk_x], f"b0{c}") for c in range(TC)]
        tk_xns = p.gd(lambda e, t0=t0: e.dma_start(
            out=xn2_s[t0:t0 + TA, :].rearrange("(c p) d -> p c d", p=128),
            in_=xn2_t[:]), wait=t_bln, stream="b0xn")
        tv_l = None
        for c in range(TC):
            for dc in range(DC):
                w = [t_bln[c]]
                if ptr_tok[0]:
                    w.append(ptr_tok[0])
                tt = p.t(lambda e, c=c, dc=dc: e.transpose(
                    ptr[0:128, 0:128], xn2_t[:, c, dc * 128:(dc + 1) * 128],
                    C["ident32"][:]), wait=w)
                tv_l = p.a(lambda e, c=c, dc=dc: e.copy(
                    out=xnF[:, dc, c * 128:(c + 1) * 128], in_=ptr[0:128, 0:128]),
                    wait=[tt])
                ptr_tok[0] = tv_l
        tk_xf = p.gd(lambda e, t0=t0: e.dma_start(
            out=xnF_s[0:DC, 0:128, t0:t0 + TA].transpose([1, 0, 2]),
            in_=xnF[:]), wait=[tv_l], stream="b0xf")
        tile_done[0] = [tk_xns, tk_xf, p.tok("T"), p.tok("V"), p.tok("A")]
    st["prev_done"] = [p.tok("D_b0xn"), p.tok("D_b0xf")]


def emit_B1(st, xnF_s, av_s):
    p, cfg, sb, bufs, C = st["p"], st["cfg"], st["sb"], st["bufs"], st["C"]
    D, DC, S, TB = cfg.D, cfg.DC, cfg.S, cfg.TB
    NH = cfg.NH
    CPT = TB // 126
    NCH = cfg.NCH
    NTB = (NCH + CPT - 1) // CPT
    W = cfg.W
    pb, pw, pstat, ptr = st["pb"], st["pw"], st["pstat"], st["ptr"]
    pb_tok, pw_tok, pstat_tok, ptr_tok = st["pb_tok"], st["pw_tok"], st["pstat_tok"], st["ptr_tok"]
    W_in = st["W_in"]

    xf_t = [sb(f"b1_xf{i}", [128, DC, 512]) for i in range(2)]
    qT = [sb(f"b1_qT{i}", [128, 512]) for i in range(2)]
    kT = [sb(f"b1_kT{i}", [128, 512]) for i in range(2)]
    vF = [sb(f"b1_vF{i}", [128, 512]) for i in range(2)]
    vT = [sb(f"b1_vT{i}", [128, 2, 65]) for i in range(CPT)]
    e_sb = [sb(f"b1_e{i}", [128, 2, 252]) for i in range(CPT)]
    av32 = [sb(f"b1_av32{i}", [128, 252]) for i in range(CPT)]
    r64 = [sb(f"b1_r64{i}", [128, 2, 252]) for i in range(CPT)]
    av_hp = [sb(f"b1_av{i}", [128, S]) for i in range(2)]
    wq_b = [[sb(f"b1_w{i}_{q}", [128, DC, 128]) for q in range(3)] for i in range(2)]
    wld_tok = [(), ()]
    avz_tok = [(), ()]
    xfu_tok = [(), ()]
    sc_tok = [(), ()]
    avp_tok = [(), ()]
    vT_tok = [()] * CPT
    e_tok = [()] * CPT
    av32_tok = [()] * CPT

    pwh_tok = [pw_tok[0], pw_tok[0]]
    tvo = p.v(lambda e: e.memset(vT[0][:, :, 64:65], 1.0))
    for i in range(1, CPT):
        tvo = p.v(lambda e, i=i: e.memset(vT[i][:, :, 64:65], 1.0))
    tz0 = p.v(lambda e: e.memset(av_hp[0][:], 0.0))
    tz1 = p.v(lambda e: e.memset(av_hp[1][:], 0.0))
    avz_tok[0] = tz0
    avz_tok[1] = tz1

    prev_d = list(st.get("prev_done", []))

    def emit_qkv(hp, k):
        hpar = hp % 2
        t0 = k * TB
        c0 = k * CPT
        ncc = min(CPT, NCH - c0)
        LL = min(((126 * (ncc - 1) + 128) + 127) // 128 * 128, S - t0)
        xpar = k % 2
        w_xf = [xfu_tok[xpar]] if xfu_tok[xpar] else []
        tk_xf = p.ld(lambda e, t0=t0, LL=LL, xpar=xpar: e.dma_start(
            out=xf_t[xpar][:, :, 0:LL],
            in_=xnF_s[0:DC, 0:128, t0:t0 + LL].transpose([1, 0, 2])),
            wait=w_xf, stream=f"b1xf{xpar}")
        part_toks = {}
        mm = None
        for part, qi in (("q", 0), ("k", 1), ("v", 2)):
            bank = pw[:, (qi % 2) * 512:(qi % 2) * 512 + 512]
            w0 = [tk_w[qi], tk_xf]
            g = pwh_tok[qi % 2]
            if g:
                w0.append(g)
            mm = None
            for dc in range(DC):
                first = (dc == 0)
                last = (dc == DC - 1)
                mm = p.t(lambda e, qi=qi, dc=dc, hpar=hpar, xpar=xpar,
                         first=first, last=last, bank=bank, LL=LL: e.matmul(
                    bank[:, 0:LL], wq_b[hpar][qi][:, dc, :],
                    xf_t[xpar][:, dc, 0:LL],
                    start=first, stop=last), wait=w0 if first else ())
            dsth = {"q": qT, "k": kT, "v": vF}[part][xpar]
            ta_h = p.a(lambda e, dsth=dsth, bank=bank, LL=LL: e.copy(
                out=dsth[:, 0:LL], in_=bank[:, 0:LL]), wait=[mm])
            pwh_tok[qi % 2] = ta_h
            part_toks[part] = ta_h
        xfu_tok[xpar] = mm
        return part_toks

    def emit_chains(hp, k, part_toks):
        hpar = hp % 2
        av = av_hp[hpar]
        t0 = k * TB
        c0 = k * CPT
        ncc = min(CPT, NCH - c0)
        xpar = k % 2
        geom = []
        for cc in range(ncc):
            cg = c0 + cc
            geom.append((cc, 126 * cc, min(128, S - 126 * cg),
                         min(63, W - 63 * cg)))
        exp_tok = [None] * ncc
        for cc, ko, KW, NW in geom:
            NQw = 4 * NW
            w_tr = [part_toks["v"]]
            if ptr_tok[0]:
                w_tr.append(ptr_tok[0])
            tt = p.t(lambda e, ko=ko, KW=KW, xpar=xpar: e.transpose(
                ptr[0:KW, 0:128], vF[xpar][:, ko:ko + KW], C["ident32"][:]),
                wait=w_tr)
            w_v = [tt]
            if vT_tok[cc]:
                w_v.append(vT_tok[cc])
            te1 = p.a(lambda e, KW=KW, cc=cc: e.copy(
                out=vT[cc][0:KW, 0, 0:64], in_=ptr[0:KW, 0:64]), wait=w_v)
            te2 = p.a(lambda e, KW=KW, cc=cc: e.copy(
                out=vT[cc][0:KW, 1, 0:64], in_=ptr[0:KW, 64:128]))
            ptr_tok[0] = te2
            vT_tok[cc] = te2
            for ha in range(2):
                sbank = pb[ha]
                w0 = [part_toks["q"], part_toks["k"]]
                if sc_tok[ha]:
                    w0.append(sc_tok[ha])
                q_gather = _qgather(qT[xpar], ha, ko, NW)
                mm2 = p.t(lambda e, ha=ha, ko=ko, KW=KW, mov=q_gather, NQw=NQw,
                          sbank=sbank, xpar=xpar: e.matmul(
                    sbank[0:KW, 0:NQw],
                    kT[xpar][64 * ha:64 * ha + 64, ko:ko + KW],
                    mov, start=True, stop=True), wait=w0)
                w_m = [mm2]
                if ha == 0 and e_tok[cc]:
                    w_m.append(e_tok[cc])
                tv_m = p.v(lambda e, ha=ha, KW=KW, NQw=NQw, NW=NW, sbank=sbank,
                           cc=cc: e.scalar_tensor_tensor(
                    out=e_sb[cc][0:KW, ha, 0:NQw].rearrange("p (j w) -> p j w", w=NW),
                    in0=sbank[0:KW, 0:NQw].rearrange(
                        "p (j w) -> p j w", w=NW),
                    scalar=float(D) ** -0.5,
                    in1=C["amask"][0:KW, :].rearrange("p (j w) -> p j w", j=4)[:, :, 0:NW],
                    op0=ALU.mult, op1=ALU.add), wait=w_m)
                sc_tok[ha] = tv_m
                ta_e = p.a(lambda e, ha=ha, KW=KW, NQw=NQw, cc=cc: e.activation(
                    out=e_sb[cc][0:KW, ha, 0:NQw], in_=e_sb[cc][0:KW, ha, 0:NQw],
                    func=AF.Exp), wait=[tv_m])
                exp_tok[cc] = ta_e
        s_tok = [None] * ncc
        for cc, ko, KW, NW in geom:
            NQw = 4 * NW
            for ha in range(2):
                abank = pb[2 + ha]
                w0a = [exp_tok[cc], vT_tok[cc]]
                if avp_tok[ha]:
                    w0a.append(avp_tok[ha])
                mma = p.t(lambda e, ha=ha, KW=KW, NQw=NQw, abank=abank, cc=cc:
                          e.matmul(
                    abank[0:65, 0:NQw], vT[cc][0:KW, ha, 0:65],
                    e_sb[cc][0:KW, ha, 0:NQw], start=True, stop=True),
                    wait=w0a)
                w_s = [mma]
                if ha == 0 and av32_tok[cc]:
                    w_s.append(av32_tok[cc])
                tv_s = p.v(lambda e, ha=ha, NQw=NQw, abank=abank, cc=cc:
                           e.tensor_copy(
                    out=r64[cc][64:65, ha, 0:NQw],
                    in_=abank[64:65, 0:NQw]), wait=w_s)
                tv_av = p.v(lambda e, ha=ha, NQw=NQw, abank=abank, cc=cc:
                            e.tensor_copy(
                    out=av32[cc][64 * ha:64 * ha + 64, 0:NQw],
                    in_=abank[0:64, 0:NQw]))
                avp_tok[ha] = tv_av
                s_tok[cc] = tv_av
        for cc, ko, KW, NW in geom:
            NQw = 4 * NW
            w_rb = [s_tok[cc]]
            if pstat_tok[0]:
                w_rb.append(pstat_tok[0])
            tbm = None
            for a_i in range(2):
                first = tbm is None
                tbm = p.t(lambda e, a_i=a_i, NQw=NQw, cc=cc: e.matmul(
                    pstat[:, a_i * NQw:(a_i + 1) * NQw],
                    C["ones32"][64:65, 0:128],
                    r64[cc][64:65, a_i, 0:NQw],
                    start=True, stop=True),
                    wait=w_rb if first else ())
            tv_rec = p.v(lambda e, NQw=NQw: e.reciprocal(
                out=pstat[:, 0:2 * NQw], in_=pstat[:, 0:2 * NQw]), wait=[tbm])
            tv_n1 = p.v(lambda e, NQw=NQw, cc=cc: e.tensor_tensor(
                out=av32[cc][0:64, 0:NQw], in0=av32[cc][0:64, 0:NQw],
                in1=pstat[0:64, 0:NQw], op=ALU.mult), wait=[tv_rec])
            tv_n = p.v(lambda e, NQw=NQw, cc=cc: e.tensor_tensor(
                out=av32[cc][64:128, 0:NQw], in0=av32[cc][64:128, 0:NQw],
                in1=pstat[64:128, NQw:2 * NQw], op=ALU.mult))
            pstat_tok[0] = tv_n
            cmb = None
            for jq in range(4):
                base = av[:, t0 + ko + jq:t0 + ko + jq + 1]
                sl_out = bass.AP(base.tensor, base.offset,
                                 [list(base.ap[0]), [2, NW]])
                w_c = [avz_tok[hpar]] if (k == 0 and cc == 0 and jq == 0
                                          and avz_tok[hpar]) else ()
                cmb = p.v(lambda e, sl_out=sl_out, jq=jq, NW=NW, cc=cc:
                          e.tensor_tensor(
                    out=sl_out, in0=sl_out,
                    in1=av32[cc][:, jq * NW:(jq + 1) * NW], op=ALU.add), wait=w_c)
            av32_tok[cc] = cmb
            e_tok[cc] = cmb

    for hp in range(NH // 2):
        hpar = hp % 2
        w_w = list(prev_d) if hp < 2 else []
        if wld_tok[hpar]:
            w_w.append(wld_tok[hpar])
        tk_w = {}
        for qi, fc in ((0, hp), (1, DC + hp), (2, 2 * DC + hp)):
            tk_w[qi] = p.ld(lambda e, qi=qi, fc=fc, hpar=hpar: e.dma_start(
                out=wq_b[hpar][qi][:],
                in_=W_in["Wqkv"][:, fc * 128:(fc + 1) * 128].rearrange(
                    "(c p) f -> p c f", p=128)),
                wait=w_w if qi == 0 else [], stream=f"b1w{hpar}{qi}")
        pend = None
        for k in range(NTB):
            pt = emit_qkv(hp, k)
            if pend is not None:
                emit_chains(hp, k - 1, pend)
            pend = pt
        emit_chains(hp, NTB - 1, pend)
        wld_tok[hpar] = xfu_tok[0]
        if xfu_tok[1]:
            wld_tok[hpar] = xfu_tok[1]
        w_st = [p.tok("V")]
        tk_av = p.gd(lambda e, hp=hp, hpar=hpar: e.dma_start(
            out=av_s[hp, :, :], in_=av_hp[hpar][:]), wait=w_st, stream=f"b1av{hpar}")
        if hp + 2 < NH // 2:
            tz = p.v(lambda e, hpar=hpar: e.memset(av_hp[hpar][:], 0.0),
                     wait=[tk_av])
            avz_tok[hpar] = tz
    pw_tok[0] = pwh_tok[1] or pwh_tok[0]
    st["prev_done"] = [p.tok("D_b1av0"), p.tok("D_b1av1")]


def emit_B2(st, x1_s, av_s, a1_s, MT):
    p, cfg, sb, bufs, C = st["p"], st["cfg"], st["sb"], st["bufs"], st["C"]
    D, DC, S = cfg.D, cfg.DC, cfg.S
    NH = cfg.NH
    TA = 512
    TC = TA // 128
    NT = S // TA
    pb, pw = st["pb"], st["pw"]
    pb_tok, pw_tok = st["pb_tok"], st["pw_tok"]
    W_in = st["W_in"]

    avb = sb("b2_av", [128, NH // 2, TA])
    x1_t = sb("b2_x1", [128, TC, D])
    mt_t = sb("b2_mt", [128, TC, 1])
    wo_b = [sb(f"b2_wo{i}", [128, D]) for i in range(2)]
    out_sl = [sb(f"b2_out{i}", [128, D]) for i in range(2)]
    wo_tok = [(), ()]
    out_tok = [(), ()]
    DH2 = min(512, D)
    NH2 = D // DH2
    prev_d = list(st.get("prev_done", []))
    tile_done = [prev_d]

    for j in range(NT):
        t0 = j * TA
        tk_av = p.ld(lambda e, t0=t0: e.dma_start(
            out=avb[:],
            in_=av_s[0:NH // 2, 0:128, t0:t0 + TA].transpose([1, 0, 2])),
            wait=tile_done[0], stream="b2av")
        tk_x = p.ld(lambda e, t0=t0: e.dma_start(
            out=x1_t[:],
            in_=x1_s[t0:t0 + TA, :].rearrange("(c p) d -> p c d", p=128)),
            wait=tile_done[0], stream="b2x")
        tk_mt = p.ld(lambda e, t0=t0: e.dma_start(
            out=mt_t[:],
            in_=MT[t0:t0 + TA, :].rearrange("(c p) d -> p c d", p=128),
            allow_slow_non_contiguous=True),
            wait=tile_done[0], stream="b2mt")
        for tcp in range(TC // 2):
            tcs = [2 * tcp, 2 * tcp + 1]
            banks = {tcs[0]: [pb[h][:, 0:DH2] for h in range(NH2)],
                     tcs[1]: [pw[:, h * DH2:(h + 1) * DH2] for h in range(NH2)]}
            mm = None
            for ccw in range(DC):
                parw = ccw % 2
                w_w = [wo_tok[parw]] if wo_tok[parw] else []
                tk_w = p.ld(lambda e, ccw=ccw, parw=parw: e.dma_start(
                    out=wo_b[parw][:],
                    in_=W_in["Wout"][ccw * 128:(ccw + 1) * 128, :]),
                    wait=w_w, stream=f"b2wo{parw}")
                kc_first = True
                for tc in tcs:
                    for half in range(NH2):
                        first = (ccw == 0)
                        last = (ccw == DC - 1)
                        w_mm = ([tk_w, tk_av] +
                                [g for g in (pb_tok[0], pb_tok[1], pw_tok[0]) if g]
                                ) if (ccw == 0 and kc_first) else (
                                [tk_w] if kc_first else ())
                        kc_first = False
                        mm = p.t(lambda e, ccw=ccw, parw=parw, tc=tc,
                                 half=half, first=first,
                                 last=last, banks=banks, DH2=DH2: e.matmul(
                            banks[tc][half][:, :],
                            avb[:, ccw, tc * 128:(tc + 1) * 128],
                            wo_b[parw][:, half * DH2:(half + 1) * DH2],
                            start=first, stop=last), wait=w_mm)
                wo_tok[parw] = mm
            for i, tc in enumerate(tcs):
                par = tc % 2
                w_o = [mm, tk_mt, tk_x]
                if out_tok[par]:
                    w_o.append(out_tok[par])
                tv_o1 = None
                for half in range(NH2):
                    tv_o1 = p.v(lambda e, tc=tc, half=half, banks=banks,
                                par=par, DH2=DH2: e.scalar_tensor_tensor(
                        out=out_sl[par][:, half * DH2:(half + 1) * DH2],
                        in0=C["bout_bc"][:, half * DH2:(half + 1) * DH2],
                        scalar=mt_t[:, tc, 0:1],
                        in1=banks[tc][half][:, :],
                        op0=ALU.mult, op1=ALU.add), wait=w_o if half == 0 else ())
                tv_o2 = p.v(lambda e, tc=tc, par=par: e.tensor_tensor(
                    out=out_sl[par][:], in0=out_sl[par][:],
                    in1=x1_t[:, tc, :], op=ALU.add))
                if i == 0:
                    for h in range(NH2):
                        pb_tok[h] = tv_o2
                else:
                    pw_tok[0] = tv_o2
                tk_o = p.gd(lambda e, t0=t0, tc=tc, par=par: e.dma_start(
                    out=a1_s[t0 + tc * 128:t0 + (tc + 1) * 128, :],
                    in_=out_sl[par][:]), wait=[tv_o2], stream=f"b2o{par}")
                out_tok[par] = tk_o
        tile_done[0] = [p.tok("V"), p.tok("T"), p.tok("D_b2o0"), p.tok("D_b2o1")]
    st["prev_done"] = [p.tok("D_b2o0"), p.tok("D_b2o1")]


def _qgather(qT, ha, ko, NW):
    base = qT[64 * ha:64 * ha + 64, ko:ko + 2]
    rs = base.ap[0][0]
    import bass_rust
    return bass.AP(base.tensor, base.offset,
                   [[rs, 64], [2, 2], [1, 2], [2, NW]])


from concourse.bass_utils import run_bass_kernel_spmd

_NC_CACHE = {}


def _get_nc():
    if "nc" not in _NC_CACHE:
        cfg = Cfg(D=1024, I=2816, S=4096, TA=512, TF=512, TB=504)
        _NC_CACHE["cfg"] = cfg
        _NC_CACHE["nc"] = build(cfg, phases=("P", "A1", "A2", "B", "C"))
    return _NC_CACHE["nc"], _NC_CACHE["cfg"]


def kernel(L0_attn_x, L0_attn_Wt, L0_attn_Wo, L0_attn_Wg, L0_attn_bg,
           L0_attn_ln1_w, L0_attn_ln1_b, L0_attn_lnc_w, L0_attn_lnc_b,
           L0_attn_lng_w, L0_attn_lng_b, L0_ffn_w_gate, L0_ffn_w_up,
           L0_ffn_w_down, L1_attn_w_qkv, L1_attn_w_out, L1_attn_b_out,
           L1_ffn_w_gate, L1_ffn_w_up, L1_ffn_w_down):
    nc, cfg = _get_nc()
    hcst = host_consts(cfg)
    f32 = lambda a: np.ascontiguousarray(np.asarray(a, dtype=np.float32))
    shared = {
        "Wt": f32(L0_attn_Wt), "Wo": f32(L0_attn_Wo), "Wg": f32(L0_attn_Wg),
        "Wqkv": f32(L1_attn_w_qkv), "Wout": f32(L1_attn_w_out),
        "L0_wgate": f32(L0_ffn_w_gate), "L0_wup": f32(L0_ffn_w_up),
        "L0_wdown": f32(L0_ffn_w_down),
        "L1_wgate": f32(L1_ffn_w_gate), "L1_wup": f32(L1_ffn_w_up),
        "L1_wdown": f32(L1_ffn_w_down),
        "ln1_w_bc": bcast128(f32(L0_attn_ln1_w)),
        "ln1_b_bc": bcast128(f32(L0_attn_ln1_b)),
        "lng_w_bc": bcast128(f32(L0_attn_lng_w)),
        "lng_b_bc": bcast128(f32(L0_attn_lng_b)),
        "bg_bc": bcast128(f32(L0_attn_bg)),
        "bout_bc": bcast128(f32(L1_attn_b_out)),
        "lnc_w_col": col128(f32(L0_attn_lnc_w), cfg.DC),
        "lnc_b_col": col128(f32(L0_attn_lnc_b), cfg.DC),
        "ident32": hcst["ident32"], "ones32": hcst["ones32"],
        "amask": hcst["amask"], "mt_vec": hcst["mt_vec"],
        "beta_col": hcst["beta_col"], "Atab": hcst["Atab"],
        "fconst": hcst["fconst"],
    }
    xs = np.ascontiguousarray(np.asarray(L0_attn_x, dtype=np.float32))
    n_cores = xs.shape[0]
    in_maps = [dict(shared, x_in=np.ascontiguousarray(xs[i]))
               for i in range(n_cores)]
    res = run_bass_kernel_spmd(nc, in_maps, core_ids=list(range(n_cores)))
    out = np.stack([np.asarray(r["out"], dtype=np.float32)
                    for r in res.results], axis=0)
    return out


# version 17
# speedup vs baseline: 1.0265x; 1.0265x over previous
import numpy as np
import concourse.bass as bass
import concourse.mybir as mybir
from concourse import library_config

F32 = mybir.dt.float32
F32R = mybir.dt.float32r
F16 = mybir.dt.float16
ALU = mybir.AluOpType
AF = mybir.ActivationFunctionType
AX = mybir.AxisListType


class Cfg:
    def __init__(self, D=1024, I=2816, S=4096, TA=512, TF=1024, TB=1008, eps_blk=1e-5,
                 eps_l0=1e-6):
        self.D = D
        self.I = I
        self.S = S
        self.TA = TA
        self.TF = TF
        self.TB = TB
        self.NH = D // 64
        self.DH = 64
        self.eps_blk = eps_blk
        self.eps_l0 = eps_l0
        self.DC = D // 128
        self.IC = I // 128
        self.NB = D // 16
        self.IB = I // 16
        assert D % 128 == 0 and I % 128 == 0 and S % TA == 0 and S % TF == 0
        assert TB % 126 == 0
        self.W = (S - 4) // 2 + 1
        self.NCH = (self.W + 62) // 63



FCONST_VALS = (1e-5, 1e-6, 0.03125, 2.0, 0.5, 3.0, 4.0, 6.0)

_DEC32_B64 = "8wQ1P+8zNT+xYjU/OpE1P4m/NT+f7TU/fRs2PyNJNj+RdjY/yKM2P8jQNj+S/TY/JSo3P4NWNz+sgjc/oK43P17aNz/qBTg/QTE4P2RcOD9Uhzg/ErI4P53cOD/2Bjk/HTE5PxJbOT/XhDk/aq45P87XOT8BATo/BCo6P9hSOj99ezo/86M6PzrMOj9T9Do/Phw7P/xDOz+Mazs/75I7Pya6Oz8w4Ts/Dgg8P8AuPD9HVTw/ons8P9KhPD/Yxzw/s+08P2QTPT/rOD0/SV49P32DPT+IqD0/as09PyTyPT+1Fj4/Hjs+P2BfPj96gz4/bKc+PzjLPj/d7j4/WxI/P7M1Pz/lWD8/8Xs/P9eePz+YwT8/NOQ/P6sGQD/9KEA/K0tAPzRtQD8aj0A/27BAP3nSQD/080A/TBVBP4E2QT+TV0E/gnhBP1CZQT/7uUE/hNpBP+z6QT8yG0I/VjtCP1pbQj89e0I//5pCP6G6Qj8i2kI/g/lCP8UYQz/nN0M/6VZDP8t1Qz+PlEM/NLNDP7nRQz8g8EM/aQ5EP5QsRD+gSkQ/jmhEP1+GRD8SpEQ/qMFEPyDfRD98/EQ/uhlFP9w2RT/hU0U/ynBFP5aNRT9HqkU/28ZFP1TjRT+x/0U/8htGPxk4Rj8kVEY/FHBGP+mLRj+kp0Y/RMNGP8reRj81+kY/hhVHP70wRz/bS0c/32ZHP8mBRz+anEc/UbdHP/DRRz917Ec/4gZIPzYhSD9xO0g/lFVIP55vSD+RiUg/a6NIPy69SD/Y1kg/a/BIP+cJST9LI0k/lzxJP81VST/rbkk/84dJP+SgST++uUk/gdJJPy7rST/FA0o/RhxKP7A0Sj8FTUo/Q2VKP2x9Sj9/lUo/fa1KP2XFSj853Uo/9vRKP58MSz8zJEs/sjtLPxxTSz9yaks/s4FLP+CYSz/4r0s//MZLP+zdSz/I9Es/kAtMP0QiTD/lOEw/cU9MP+tlTD9RfEw/o5JMP+OoTD8Pv0w/KNVMPy/rTD8iAU0/AxdNP9EsTT+NQk0/NlhNP8xtTT9Rg00/w5hNPyOuTT9xw00/rdhNP9ftTT/wAk4/9xdOP+wsTj/QQU4/olZOP2NrTj8TgE4/spROPz+pTj+8vU4/J9JOP4LmTj/M+k4/Bg9PPy4jTz9HN08/T0tPP0ZfTz8tc08/BIdPP8uaTz+Crk8/KcJPP8DVTz9H6U8/vvxPPyYQUD9+I1A/xzZQPwBKUD8qXVA/RHBQP1CDUD9MllA/OalQPxa8UD/lzlA/puFQP1f0UD/6BlE/jRlRPxMsUT+JPlE/8VBRP0tjUT+WdVE/1IdRPwKaUT8jrFE/Nr5RPzrQUT8x4lE/GvRRP/UFUj/CF1I/gilSPzM7Uj/XTFI/bl5SP/dvUj9zgVI/4pJSP0OkUj+WtVI/3cZSPxfYUj9D6VI/Y/pSP3ULUz97HFM/dC1TP2A+Uz9AT1M/E2BTP9lwUz+SgVM/P5JTP+CiUz91s1M//MNTP3jUUz/o5FM/S/VTP6IFVD/tFVQ/LCZUP182VD+HRlQ/olZUP7FmVD+1dlQ/rYZUP5mWVD96plQ/T7ZUPxnGVD/X1VQ/iuVUPzH1VD/OBFU/XxRVP+QjVT9fM1U/zkJVPzJSVT+LYVU/2nBVPx2AVT9Vj1U/g55VP6atVT+9vFU/ystVP83aVT/F6VU/svhVP5UHVj9uFlY/OyVWP/8zVj+4QlY/Z1FWPwtgVj+lblY/NX1WP7uLVj83mlY/qKhWPxC3Vj9uxVY/wdNWPwviVj9L8FY/gf5WP60MVz/QGlc/6ChXP/c2Vz/9RFc/+VJXP+tgVz/Ublc/s3xXP4mKVz9VmFc/GKZXP9KzVz+CwVc/Kc9XP8fcVz9c6lc/5/dXP2oFWD/jElg/VCBYP7stWD8ZO1g/bkhYP7tVWD/+Ylg/OXBYP2t9WD+Uilg/tZdYP82kWD/csVg/4r5YP+DLWD/V2Fg/wuVYP6byWD+C/1g/VQxZPyAZWT/jJVk/nTJZP08/WT/5S1k/mlhZPzNlWT/EcVk/TX5ZP86KWT9Hl1k/t6NZPyCwWT+AvFk/2chZPyrVWT9y4Vk/s+1ZP+z5WT8eBlo/RxJaP2geWj+CKlo/lTZaP59CWj+iTlo/nVpaP5FmWj99clo/Yn5aPz+KWj8Ullo/4qFaP6mtWj9puVo/IMVaP9HQWj963Fo/HOhaP7fzWj9L/1o/1wpbP1wWWz/aIVs/US1bP8E4Wz8pRFs/i09bP+VaWz85Zls/hXFbP8t8Wz8KiFs/QZNbP3KeWz+cqVs/wLRbP9y/Wz/yyls/AdZbPwnhWz8K7Fs/BfdbP/kBXD/nDFw/zRdcP64iXD+HLVw/WjhcPydDXD/tTVw/rVhcP2ZjXD8Zblw/xXhcP2uDXD8Ljlw/pJhcPzejXD/ErVw/SrhcP8rCXD9EzVw/uNdcPyXiXD+N7Fw/7vZcP0kBXT+eC10/7RVdPzYgXT95Kl0/tTRdP+w+XT8dSV0/SFNdP2xdXT+LZ10/pXFdP7h7XT/FhV0/zY9dP86ZXT/Ko10/wK1dP7G3XT+bwV0/gMtdP1/VXT85310/DeldP9vyXT+j/F0/ZgZePyQQXj/cGV4/jiNePzstXj/iNl4/hEBePyBKXj+3U14/SF1eP9RmXj9acF4/3HleP1eDXj/OjF4/P5ZeP6ufXj8RqV4/crJeP867Xj8lxV4/ds5eP8PXXj8K4V4/TOpeP4jzXj/A/F4/8gVfPyAPXz9IGF8/ayFfP4kqXz+iM18/tjxfP8VFXz/PTl8/1FdfP9RgXz/QaV8/xnJfP7d7Xz+khF8/i41fP26WXz9Ln18/JKhfP/mwXz/IuV8/ksJfP1jLXz8Z1F8/1dxfP43lXz9A7l8/7vZfP5f/Xz88CGA/3BBgP3cZYD8OImA/oSpgPy4zYD+3O2A/PERgP7xMYD83VWA/rl1gPyBmYD+ObmA/93ZgP1x/YD+8h2A/GJBgP3CYYD/DoGA/EqlgP1yxYD+iuWA/5MFgPyHKYD9a0mA/jtpgP7/iYD/r6mA/EvNgPzb7YD9VA2E/cAthP4YTYT+ZG2E/pyNhP7ErYT+3M2E/uTthP7dDYT+wS2E/pVNhP5dbYT+EY2E/bWthP1JzYT8ye2E/D4NhP+iKYT+9kmE/jZphP1qiYT8jqmE/6LFhP6i5YT9lwWE/HslhP9PQYT+E2GE/MeBhP9vnYT+A72E/IfdhP7/+YT9ZBmI/7g1iP4EVYj8PHWI/mSRiPyAsYj+jM2I/IjtiP51CYj8VSmI/iFFiP/lYYj9lYGI/zmdiPzJvYj+UdmI/8X1iP0uFYj+ijGI/9JNiP0ObYj+PomI/1qliPxuxYj9buGI/mL9iP9LGYj8HzmI/OtViP2jcYj+U42I/vOpiP+DxYj8B+WI/HgBjPzgHYz9ODmM/YRVjP3AcYz98I2M/hSpjP4oxYz+LOGM/iT9jP4RGYz98TWM/cFRjP2FbYz9OYmM/OGljPx9wYz8Cd2M/431jP7+EYz+Zi2M/b5JjP0KZYz8SoGM/3qZjP6etYz9ttGM/MLtjP+/BYz+ryGM/ZM9jPxrWYz/N3GM/fONjPynqYz/S8GM/ePdjPxv+Yz+6BGQ/VwtkP/ARZD+HGGQ/Gh9kP6olZD83LGQ/wTJkP0g5ZD/MP2Q/TUZkP8pMZD9FU2Q/vVlkPzJgZD+jZmQ/Em1kP35zZD/meWQ/TIBkP6+GZD8PjWQ/bJNkP8aZZD8doGQ/caZkP8KsZD8Qs2Q/W7lkP6S/ZD/pxWQ/LMxkP2zSZD+o2GQ/495kPxrlZD9O62Q/gPFkP673ZD/a/WQ/AwRlPyoKZT9NEGU/bhZlP4wcZT+nImU/vyhlP9UuZT/nNGU/+DplPwVBZT8QR2U/GE1lPx1TZT8fWWU/H19lPxxlZT8Wa2U/DnFlPwN3ZT/2fGU/5YJlP9KIZT+9jmU/pZRlP4qaZT9soGU/TKZlPymsZT8EsmU/3LdlP7K9ZT+Ew2U/VcllPyLPZT/u1GU/ttplP3zgZT9A5mU/AexlP7/xZT9792U/Nf1lP+sCZj+gCGY/Ug5mPwEUZj+uGWY/WR9mPwElZj+mKmY/STBmP+k1Zj+IO2Y/I0FmP71GZj9TTGY/6FFmP3pXZj8JXWY/lmJmPyFoZj+qbWY/MHNmP7N4Zj80fmY/s4NmPzCJZj+qjmY/IpRmP5eZZj8Kn2Y/e6RmP+mpZj9Vr2Y/v7RmPya6Zj+Mv2Y/78RmP0/KZj+tz2Y/CdVmP2PaZj+732Y/EOVmP2PqZj+z72Y/AvVmP076Zj+Y/2Y/4ARnPyUKZz9pD2c/qhRnP+gZZz8lH2c/XyRnP5gpZz/OLmc/AjRnPzM5Zz9jPmc/kENnP7tIZz/kTWc/C1NnPzBYZz9TXWc/c2JnP5JnZz+ubGc/yHFnP+B2Zz/2e2c/CYFnPxuGZz8ri2c/OJBnP0SVZz9Nmmc/VJ9nP1mkZz9cqWc/Xa5nP1yzZz9ZuGc/VL1nP03CZz9Ex2c/OcxnPyzRZz8c1mc/C9tnP/jfZz/i5Gc/y+lnP7LuZz+X82c/efhnP1r9Zz85Amg/FgdoP/ALaD/JEGg/oBVoP3UaaD9IH2g/GSRoP+goaD+2LWg/gTJoP0o3aD8SPGg/10BoP5tFaD9dSmg/HE9oP9pTaD+WWGg/UF1oPwhiaD+/Zmg/c2toPyZwaD/XdGg/hXloPzJ+aD/dgmg/h4doPy6MaD/UkGg/d5VoPxmaaD+5nmg/WKNoP/SnaD+PrGg/KLFoP7+1aD9Uumg/575oP3nDaD8JyGg/l8xoPyPRaD+t1Wg/NtpoP73eaD9C42g/xedoP0fsaD/H8Gg/RfVoP8H5aD88/mg/tQJpPywHaT+hC2k/FRBpP4cUaT/3GGk/Zh1pP9IhaT89Jmk/pyppPw8vaT91M2k/2TdpPzs8aT+cQGk//ERpP1lJaT+1TWk/D1JpP2hWaT+/Wmk/FF9pP2djaT+5Z2k/CmxpP1hwaT+ldGk/8XhpPzp9aT+CgWk/yYVpPw6KaT9Rjmk/k5JpP9OWaT8Rm2k/Tp9pP4mjaT/Cp2k/+qtpPzGwaT9mtGk/mbhpP8u8aT/7wGk/KcVpP1bJaT+BzWk/q9FpP9PVaT/62Wk/H95pP0PiaT9l5mk/heppP6TuaT/C8mk/3vZpP/j6aT8R/2k/KANqPz4Haj9SC2o/ZQ9qP3YTaj+GF2o/lBtqP6Efaj+sI2o/tidqP74raj/FL2o/yjNqP843aj/QO2o/0T9qP9BDaj/OR2o/y0tqP8ZPaj+/U2o/uFdqP65baj+jX2o/l2NqP4lnaj96a2o/am9qP1hzaj9Ed2o/L3tqPxl/aj8Bg2o/6IZqP86Kaj+yjmo/lJJqP3WWaj9Vmmo/NJ5qPxGiaj/spWo/x6lqP5+taj93sWo/TbVqPyK5aj/1vGo/x8BqPw=="


def _dec32_table():
    import base64 as _b64
    return np.frombuffer(_b64.b64decode(_DEC32_B64), dtype=np.float32).copy()


def host_consts(cfg):
    D, S = cfg.D, cfg.S
    c = {}
    c["fconst"] = np.broadcast_to(np.array(FCONST_VALS, np.float32), (128, 8)).copy()
    dec = _dec32_table()[:D]
    beta = np.float32(1.0) - dec
    c["beta_col"] = beta.reshape(cfg.DC, 128).T.copy()
    NL = 12
    A = dec.copy()
    at = np.empty((cfg.DC, NL, 128), np.float32)
    for l in range(NL):
        at[:, l, :] = A.reshape(cfg.DC, 128)
        A = A * A
    c["Atab"] = at.reshape(cfg.DC * NL, 128).T.copy()
    c["ident32"] = np.eye(128, dtype=np.float32)
    c["ones32"] = np.ones((128, 128), dtype=np.float32)
    m = np.full((128, 4, 63), -120.0, dtype=np.float32)
    for w in range(63):
        m[2 * w:2 * w + 4, :, w] = 0.0
    c["amask"] = m.reshape(128, 252).copy()
    mt = np.full((S, 1), 2.0, dtype=np.float32)
    mt[[0, 1, S - 2, S - 1], 0] = 1.0
    c["mt_vec"] = mt
    return c


def bcast128(v):
    return np.broadcast_to(np.asarray(v, np.float32), (128, v.shape[-1])).copy()


def col128(v, DC):
    return np.asarray(v, np.float32).reshape(DC, 128).T.copy()



ENGS = ("sync", "tensor", "vector", "scalar", "gpsimd")


class Prog:
    def __init__(self, nc):
        self.nc = nc
        self.q = {e: [] for e in ENGS}
        self.cnt = {}
        self.sem_names = set()

    def tok(self, sem):
        return (sem, self.cnt.get(sem, 0))

    def emit(self, eng, fn, wait=(), inc=None, amt=1):
        w = tuple((s, v) for t in wait if t for (s, v) in [t] if v > 0)
        self.q[eng].append((w, fn, inc, amt))
        if inc is not None:
            self.sem_names.add(inc)
            self.cnt[inc] = self.cnt.get(inc, 0) + amt
            return (inc, self.cnt[inc])
        return None

    def t(self, fn, wait=(), inc="T"):
        return self.emit("tensor", fn, wait, inc)

    def v(self, fn, wait=(), inc="V"):
        w = tuple(wait) + ((self.tok("V"),) if self.cnt.get("V", 0) else ())
        return self.emit("vector", fn, w, inc)

    def a(self, fn, wait=(), inc="A"):
        w = tuple(wait) + ((self.tok("A"),) if self.cnt.get("A", 0) else ())
        return self.emit("scalar", fn, w, inc)

    def g(self, fn, wait=(), inc="G"):
        w = tuple(wait) + ((self.tok("G"),) if self.cnt.get("G", 0) else ())
        return self.emit("gpsimd", fn, w, inc)

    def dma(self, eng, stream, fn, wait=()):
        return self.emit(eng, fn, wait, "D_" + stream, 16)

    def ld(self, fn, wait=(), stream="ld"):
        return self.dma("sync", stream, fn, wait)

    def gd(self, fn, wait=(), stream="gd"):
        return self.dma("gpsimd", stream, fn, wait)

    def sd(self, fn, wait=(), stream="sd"):
        return self.dma("scalar", stream, fn, wait)

    def drain(self, eng):
        pass

    def barrier(self):
        toks = [self.tok(s) for s in ("T", "V", "A", "G")]
        toks += [self.tok(s) for s in sorted(self.sem_names) if s.startswith("D_")]
        toks = [t for t in toks if t[1] > 0]
        for e in ENGS:
            self.emit(e, lambda en: en.nop(), tuple(toks), None)

    def build(self, sems, block):
        for ename in ENGS:
            section = getattr(block, {"sync": "sync", "tensor": "tensor",
                                      "vector": "vector", "scalar": "scalar",
                                      "gpsimd": "gpsimd"}[ename])

            def mk(ename):
                def body(eng):
                    waited = {}
                    for w, fn, inc, amt in self.q[ename]:
                        for (s, v) in w:
                            if waited.get(s, 0) < v:
                                eng.wait_ge(sems[s], v)
                                waited[s] = v
                        h = fn(eng)
                        if inc is not None:
                            assert h is not None, f"inc={inc} on fn returning None"
                            h.then_inc(sems[inc], amt)
                return body

            section(mk(ename))



def mk_ln_bufs(sb, bufs, tag):
    if f"bn6_{tag}" not in bufs:
        bufs[f"bn6_{tag}"] = sb(f"lnb6_{tag}", [128, 12])
        bufs[f"bn2_{tag}"] = sb(f"lnb2_{tag}", [128, 2])
        bufs[f"rs_{tag}"] = sb(f"lnbr_{tag}", [128, 1])


def emit_layernorm_T(p, bufs, src_ap, dst_ap, n, eps, wait, tag,
                     gamma_bc=None, beta_bc=None, out2=None):
    bn6, bn2, s_rs = bufs[f"bn6_{tag}"], bufs[f"bn2_{tag}"], bufs[f"rs_{tag}"]
    nhalf = n // 2
    tv1 = p.v(lambda e: e.bn_stats(out=bn6[:, 0:6], in_=src_ap[:, 0:nhalf]), wait=wait)
    tv2 = p.v(lambda e: e.bn_stats(out=bn6[:, 6:12], in_=src_ap[:, nhalf:n]))
    tv3 = p.v(lambda e: e.bn_aggr(out=bn2[:], in_=bn6[:]))
    ta1 = p.a(lambda e: e.activation(out=s_rs[:], in_=bn2[:, 1:2], func=AF.Sqrt,
                                     bias=float(eps), scale=1.0), wait=list(wait) + [tv3])
    tv5 = p.v(lambda e: e.reciprocal(out=s_rs[:], in_=s_rs[:]), wait=[ta1])
    tv6 = p.v(lambda e: e.tensor_scalar(out=dst_ap, in0=src_ap, scalar1=bn2[:, 0:1],
                                        scalar2=s_rs[:, 0:1], op0=ALU.subtract,
                                        op1=ALU.mult))
    if gamma_bc is not None:
        p.drain("vector")
        tv7 = p.v(lambda e: e.tensor_tensor(out=dst_ap, in0=dst_ap, in1=gamma_bc,
                                            op=ALU.mult))
        p.drain("vector")
        tv8 = p.v(lambda e: e.tensor_tensor(out=dst_ap, in0=dst_ap, in1=beta_bc,
                                            op=ALU.add))
        p.drain("vector")
        return tv8
    p.drain("vector")
    return tv6


def emit_quant(p, bufs, src_ap, qs_ap, sch_ap, scl_ap, n, tag, wait, out32=None):
    nb = n // 16
    ax, sg = bufs[f"qax_{tag}"], bufs[f"qsg_{tag}"]
    u = ax
    qa, qb = bufs[f"qqa_{tag}"], bufs[f"qqb_{tag}"]
    bm, sc, rcp = bufs[f"qbm_{tag}"], bufs[f"qsc_{tag}"], bufs[f"qrcp_{tag}"]
    src3 = src_ap.rearrange("p (b s) -> p b s", s=16)

    ta1 = p.a(lambda e: e.activation(out=ax[:, 0:n], in_=src_ap, func=AF.Abs), wait=wait)
    ta2 = p.a(lambda e: e.activation(out=sg[:, 0:n], in_=src_ap, func=AF.Sign))
    p.drain("scalar")
    tv1 = p.v(lambda e: e.tensor_reduce(out=bm[:, 0:nb], in_=src3, axis=AX.X,
                                        op=ALU.max, apply_absolute_value=True),
              wait=wait)
    p.drain("vector")
    tv2 = p.v(lambda e: e.tensor_scalar(out=sc[:, 0:nb], in0=bm[:, 0:nb],
                                        scalar1=bufs["rcp6"][:, 0:1], scalar2=1e-12,
                                        op0=ALU.mult, op1=ALU.max))
    p.drain("vector")
    tv3 = p.v(lambda e: e.reciprocal(out=rcp[:, 0:nb], in_=sc[:, 0:nb]))
    p.drain("vector")
    tv7 = p.v(lambda e: e.tensor_tensor(
        out=u[:, 0:n].rearrange("p (b s) -> p b s", s=16),
        in0=ax[:, 0:n].rearrange("p (b s) -> p b s", s=16),
        in1=rcp[:, 0:nb].unsqueeze(-1).broadcast_to([128, nb, 16]), op=ALU.mult),
        wait=[ta1])
    p.drain("vector")
    p.v(lambda e: e.tensor_scalar(out=qa[:, 0:n], in0=u[:, 0:n], scalar1=0.25,
                                  scalar2=None, op0=ALU.is_gt))
    p.drain("vector")
    for th in (0.75, 1.25, 1.75):
        p.v(lambda e, th=th: e.scalar_tensor_tensor(
            out=qa[:, 0:n], in0=u[:, 0:n], scalar=th, in1=qa[:, 0:n],
            op0=ALU.is_gt, op1=ALU.add))
        p.drain("vector")
    p.v(lambda e: e.tensor_scalar(out=qb[:, 0:n], in0=u[:, 0:n], scalar1=2.5,
                                  scalar2=None, op0=ALU.is_gt))
    p.drain("vector")
    for th in (3.5, 5.0, 5.0):
        p.v(lambda e, th=th: e.scalar_tensor_tensor(
            out=qb[:, 0:n], in0=u[:, 0:n], scalar=th, in1=qb[:, 0:n],
            op0=ALU.is_gt, op1=ALU.add))
        p.drain("vector")
    p.v(lambda e: e.scalar_tensor_tensor(out=qb[:, 0:n], in0=qa[:, 0:n], scalar=0.5,
                                         in1=qb[:, 0:n], op0=ALU.mult, op1=ALU.add))
    p.drain("vector")
    p.v(lambda e: e.tensor_tensor(out=qb[:, 0:n], in0=qb[:, 0:n], in1=sg[:, 0:n],
                                  op=ALU.mult), wait=[ta2])
    p.drain("vector")
    tvq = p.v(lambda e: e.tensor_tensor(
        out=out32.rearrange("p (b s) -> p b s", s=16),
        in0=qb[:, 0:n].rearrange("p (b s) -> p b s", s=16),
        in1=sc[:, 0:nb].unsqueeze(-1).broadcast_to([128, nb, 16]), op=ALU.mult))
    p.drain("vector")
    return tvq




def build(cfg, phases=("P", "A1", "A2", "B", "C"), debug_outputs=()):
    nc = bass.Bass("TRN2", target_bir_lowering=False)
    D, I, S = cfg.D, cfg.I, cfg.S
    DC, IC = cfg.DC, cfg.IC

    X = nc.dram_tensor("x_in", [S, D], F32, kind="ExternalInput").ap()
    Wt = nc.dram_tensor("Wt", [D, D], F32, kind="ExternalInput").ap()
    Wo = nc.dram_tensor("Wo", [D, D], F32, kind="ExternalInput").ap()
    Wg = nc.dram_tensor("Wg", [2 * D, D], F32, kind="ExternalInput").ap()
    WqkvIn = nc.dram_tensor("Wqkv", [D, 3 * D], F32, kind="ExternalInput").ap()
    WoutIn = nc.dram_tensor("Wout", [D, D], F32, kind="ExternalInput").ap()
    FW = {}
    for L in (0, 1):
        FW[(L, "g")] = nc.dram_tensor(f"L{L}_wgate", [I, D], F32, kind="ExternalInput").ap()
        FW[(L, "u")] = nc.dram_tensor(f"L{L}_wup", [I, D], F32, kind="ExternalInput").ap()
        FW[(L, "d")] = nc.dram_tensor(f"L{L}_wdown", [D, I], F32, kind="ExternalInput").ap()
    CIN = {}
    for nm, shape in [("ln1_w_bc", [128, D]), ("ln1_b_bc", [128, D]),
                      ("lng_w_bc", [128, D]), ("lng_b_bc", [128, D]),
                      ("bg_bc", [128, D]), ("bout_bc", [128, D]),
                      ("lnc_w_col", [128, DC]), ("lnc_b_col", [128, DC]),
                      ("beta_col", [128, DC]), ("Atab", [128, DC * 12]),
                      ("ident32", [128, 128]), ("ones32", [128, 128]), ("amask", [128, 252]),
                      ("fconst", [128, 8])]:
        CIN[nm] = nc.dram_tensor(nm, shape, F32, kind="ExternalInput").ap()

    MT = nc.dram_tensor("mt_vec", [S, 1], F32, kind="ExternalInput").ap()
    OUT = nc.dram_tensor("out", [S, D], F32, kind="ExternalOutput").ap()

    def scr(name, shape, dt=F32):
        return nc.dram_tensor(name, shape, dt, kind="Internal").ap()

    xn_s = scr("xn_s", [S, D])
    u_s = scr("u_s", [DC, 128, S])
    cx_s = scr("cx_s", [DC, 128, S])
    a0_s = scr("a0_s", [S, D])
    x1_s = scr("x1_s", [S, D])
    xn2_s = scr("xn2_s", [S, D])
    a1_s = scr("a1_s", [S, D])
    FWp = {}
    for L in (0, 1):
        FWp[(L, "g")] = scr(f"L{L}_wgate_p", [DC, 128, I], F32)
        FWp[(L, "u")] = scr(f"L{L}_wup_p", [DC, 128, I], F32)
        FWp[(L, "d")] = scr(f"L{L}_wdown_p", [IC, 128, D], F32)

    p = Prog(nc)
    import contextlib
    stack = contextlib.ExitStack()

    def sb(name, shape, dt=F32):
        return stack.enter_context(nc.sbuf_tensor(name, shape, dt))

    def ps(name, shape, dt=F32):
        return stack.enter_context(nc.psum_tensor(name, shape, dt))

    bufs = {}

    C = {}
    for nm in CIN:
        shape = list(CIN[nm].shape)
        C[nm] = sb("c_" + nm, shape, F32)
    for nm in CIN:
        p.ld(lambda e, nm=nm: e.dma_start(out=C[nm][:], in_=CIN[nm][:]), stream="cst")
    tok_const = [p.tok("D_cst")]
    for i, val in enumerate(FCONST_VALS):
        nc.const_aps.register(F32, float(val), C["fconst"][:, i:i + 1])
    for _e in ENGS:
        p.emit(_e, lambda e: e.nop(), tuple(tok_const), None)

    bufs["rcp6"] = sb("c_rcp6", [128, 1])
    p.v(lambda e: e.memset(bufs["rcp6"][:], 6.0))
    p.drain("vector")
    p.v(lambda e: e.reciprocal(out=bufs["rcp6"][:], in_=bufs["rcp6"][:]))
    p.drain("vector")

    pb = [ps(f"pb{i}", [128, 512]) for i in range(4)]
    pw = ps("pw", [128, 1024])
    pstat = ps("pstat", [128, 512])
    ptr = ps("ptr", [128, 512])
    pb_tok = [() for _ in range(4)]
    pw_tok = [()]
    pstat_tok = [()]
    ptr_tok = [()]

    W_in = dict(Wt=Wt, Wo=Wo, Wg=Wg, Wqkv=WqkvIn, Wout=WoutIn)
    state = dict(p=p, nc=nc, cfg=cfg, sb=sb, bufs=bufs, C=C, pb=pb, pw=pw,
                 pstat=pstat, ptr=ptr, pb_tok=pb_tok, pw_tok=pw_tok,
                 pstat_tok=pstat_tok, ptr_tok=ptr_tok, tok_const=tok_const,
                 W_in=W_in, FWp=FWp)

    import contextlib as _ctx

    def scoped(fn, *args):
        p.barrier()
        with _ctx.ExitStack() as ph_stack:
            def sb_ph(name, shape, dt=F32):
                return ph_stack.enter_context(nc.sbuf_tensor(name, shape, dt))
            state["sb"] = sb_ph
            fn(state, *args)
        state["sb"] = sb

    if "P" in phases:
        scoped(emit_preprocess, FW, FWp)
    if "A1" in phases:
        scoped(emit_A1a, X, xn_s, u_s)
        scoped(emit_A1b, u_s, cx_s)
        scoped(emit_A1c, X, xn_s, cx_s, a0_s)
    if "A2" in phases:
        scoped(emit_ffn_phase, xn_s, a0_s, x1_s, 0, FWp)
    if "B" in phases:
        scoped(emit_B0, x1_s, xn2_s, u_s)
        scoped(emit_B1, u_s, cx_s)
        scoped(emit_B2, x1_s, cx_s, a1_s, MT)
    if "C" in phases:
        scoped(emit_ffn_phase, xn2_s, a1_s, OUT, 1, FWp)

    dbg_map = {"xn_s": xn_s, "a0_s": a0_s, "x1_s": x1_s, "xn2_s": xn2_s, "a1_s": a1_s}
    for nm in debug_outputs:
        src = dbg_map[nm]
        dbg = nc.dram_tensor("dbg_" + nm, [S, D], F32, kind="ExternalOutput").ap()
        tbuf = sb("dbgbuf_" + nm, [128, D])
        for r in range(S // 128):
            wdbg = [p.tok("T"), p.tok("V"), p.tok("A"), p.tok("G")]
            wdbg += [p.tok(s) for s in sorted(p.sem_names) if s.startswith("D_")]
            tk = p.ld(lambda e, r=r, src=src, tbuf=tbuf: e.dma_start(
                out=tbuf[:], in_=src[r * 128:(r + 1) * 128, :]),
                wait=wdbg, stream="dbgl")
            p.gd(lambda e, r=r, dbg=dbg, tbuf=tbuf: e.dma_start(
                out=dbg[r * 128:(r + 1) * 128, :], in_=tbuf[:]), wait=[tk],
                stream="dbgs")

    final_waits = [p.tok(s) for s in sorted(p.sem_names) if s.startswith("D_")]
    p.emit("gpsimd", lambda e: e.nop(), tuple(final_waits), None)

    sems = {}
    for s_name in sorted(p.sem_names | {"T", "V", "A", "G"}):
        sems[s_name] = stack.enter_context(nc.semaphore(s_name))
    with stack:
        with nc.Block() as block:
            p.build(sems, block)
    return nc



def emit_preprocess(st, FW, FWp):
    p, cfg, sb, bufs, C = st["p"], st["cfg"], st["sb"], st["bufs"], st["C"]
    D, I, DC, IC = cfg.D, cfg.I, cfg.DC, cfg.IC
    ptr, ptr_tok = st["ptr"], st["ptr_tok"]
    nmax = max(D, I)
    wb32 = [sb(f"pp_w32_{i}", [128, nmax]) for i in range(2)]
    wq32 = [sb(f"pp_wq32_{i}", [128, nmax]) for i in range(2)]
    tr32 = [sb(f"pp_tr32_{i}", [128, max(DC, IC), 128]) for i in range(2)]
    for par in (0, 1):
        for nm, shape, dt in [(f"qax_pp{par}", [128, nmax], F32),
                              (f"qsg_pp{par}", [128, nmax], F32),
                              (f"qqa_pp{par}", [128, nmax], F32),
                              (f"qqb_pp{par}", [128, nmax], F32),
                              (f"qbm_pp{par}", [128, nmax // 16], F32),
                              (f"qsc_pp{par}", [128, nmax // 16], F32),
                              (f"qrcp_pp{par}", [128, nmax // 16], F32)]:
            bufs[nm] = sb(nm, shape, dt)

    q_tok = [(), ()]
    st_tok = [(), ()]
    ctr = [0]
    ready = {}

    def do_ffn_weight(src, dst, nrows, kdim, name):
        nrc = nrows // 128
        kcc = kdim // 128
        for rc in range(nrc):
            par = ctr[0] % 2
            ctr[0] += 1
            w_l = []
            if q_tok[par]:
                w_l.append(q_tok[par])
            tk = p.ld(lambda e, src=src, rc=rc, par=par, kdim=kdim: e.dma_start(
                out=wb32[par][:, 0:kdim], in_=src[rc * 128:(rc + 1) * 128, 0:kdim]),
                wait=w_l, stream=f"ppl{par}")
            tq = emit_quant(p, bufs, wb32[par][:, 0:kdim], None,
                            None, None, kdim, f"pp{par}", [tk],
                            out32=wq32[par][:, 0:kdim])
            q_tok[par] = tq
            te = None
            for kc in range(kcc):
                w_tr = [tq]
                if ptr_tok[0]:
                    w_tr.append(ptr_tok[0])
                tt = p.t(lambda e, kc=kc, par=par: e.transpose(
                    ptr[0:128, 0:128], wq32[par][:, kc * 128:(kc + 1) * 128],
                    C["ident32"][:]), wait=w_tr)
                w_te = [tt]
                if kc == 0 and st_tok[par]:
                    w_te.append(st_tok[par])
                te = p.a(lambda e, kc=kc, par=par: e.copy(
                    out=tr32[par][:, kc, :], in_=ptr[0:128, 0:128]), wait=w_te)
                ptr_tok[0] = te
            tg = p.gd(lambda e, rc=rc, dst=dst, kcc=kcc, par=par: e.dma_start(
                out=dst[0:kcc, 0:128, rc * 128:(rc + 1) * 128].transpose([1, 0, 2]),
                in_=tr32[par][:, 0:kcc, :]),
                wait=[te], stream=f"pps{par}")
            st_tok[par] = tg
        ready[name] = (p.tok("D_pps0"), p.tok("D_pps1"))

    do_ffn_weight(FW[(0, "g")], FWp[(0, "g")], I, D, "L0g")
    do_ffn_weight(FW[(0, "u")], FWp[(0, "u")], I, D, "L0u")
    do_ffn_weight(FW[(0, "d")], FWp[(0, "d")], D, I, "L0d")
    do_ffn_weight(FW[(1, "g")], FWp[(1, "g")], I, D, "L1g")
    do_ffn_weight(FW[(1, "u")], FWp[(1, "u")], I, D, "L1u")
    do_ffn_weight(FW[(1, "d")], FWp[(1, "d")], D, I, "L1d")
    st["pp_done"] = [p.tok("D_pps0"), p.tok("D_pps1"), p.tok("V"), p.tok("A"), p.tok("T")]



def emit_A1a(st, X, xn_s, u_s):
    p, cfg, sb, bufs, C = st["p"], st["cfg"], st["sb"], st["bufs"], st["C"]
    D, DC, TA, S = cfg.D, cfg.DC, cfg.TA, cfg.S
    TC = TA // 128
    NT = S // TA
    pb, ptr = st["pb"], st["ptr"]
    pb_tok, ptr_tok = st["pb_tok"], st["ptr_tok"]
    W_in = st["W_in"]

    x_t = sb("a1_x", [128, TC, D])
    xn_t = sb("a1_xn", [128, TC, D])
    lx_t = sb("a1_lx", [128, TC, D])
    xf = sb("a1_xf", [128, DC, TA])
    u_sb = [sb(f"a1_u{i}", [128, TA]) for i in range(2)]
    for c in range(TC):
        mk_ln_bufs(sb, bufs, f"a1{c}")
    wt_buf = [sb(f"a1_wt{i}", [128, DC, 128]) for i in range(2)]
    wt_tok = [(), ()]
    us_tok = [(), ()]

    tile_done = [st.get("pp_done", [])[:] or []]

    for j in range(NT):
        t0 = j * TA
        tk_x = p.ld(lambda e, t0=t0: e.dma_start(
            out=x_t[:],
            in_=X[t0:t0 + TA, :].rearrange("(c p) d -> p c d", p=128)),
            wait=tile_done[0], stream="a1x")
        t_bln = [emit_layernorm_T(p, bufs, x_t[:, c, :],
                                  xn_t[:, c, :], D, cfg.eps_blk,
                                  [tk_x], f"a1{c}") for c in range(TC)]
        tk_xns = p.gd(lambda e, t0=t0: e.dma_start(
            out=xn_s[t0:t0 + TA, :].rearrange("(c p) d -> p c d", p=128),
            in_=xn_t[:]), wait=t_bln, stream="a1xn")
        t_ln1 = [emit_layernorm_T(p, bufs, xn_t[:, c, :], lx_t[:, c, :], D,
                                  cfg.eps_l0, [t_bln[c]], f"a1{c}",
                                  gamma_bc=C["ln1_w_bc"][:], beta_bc=C["ln1_b_bc"][:])
                 for c in range(TC)]
        tv_lx = t_ln1[-1]
        for c in range(TC):
            for dc in range(DC):
                w = [t_ln1[c]]
                if ptr_tok[0]:
                    w.append(ptr_tok[0])
                tt = p.t(lambda e, c=c, dc=dc: e.transpose(
                    ptr[0:128, 0:128], lx_t[:, c, dc * 128:(dc + 1) * 128],
                    C["ident32"][:]), wait=w)
                ta_h = p.a(lambda e, c=c, dc=dc: e.copy(
                    out=xf[:, dc, c * 128:(c + 1) * 128], in_=ptr[0:128, 0:128]),
                    wait=[tt])
                ptr_tok[0] = ta_h
        tv_xf = ta_h
        for hc in range(DC):
            par = hc % 2
            w_w = [wt_tok[par]] if wt_tok[par] else []
            tk_w = p.ld(lambda e, hc=hc, par=par: e.dma_start(
                out=wt_buf[par][:],
                in_=W_in["Wt"][:, hc * 128:(hc + 1) * 128].rearrange(
                    "(c p) h -> p c h", p=128)),
                wait=w_w, stream=f"a1wt{par}")
            bank = pb[par]
            w0 = [tk_w, tv_xf]
            if pb_tok[par]:
                w0.append(pb_tok[par])
            mm = None
            for dc in range(DC):
                first = (dc == 0)
                last = (dc == DC - 1)
                mm = p.t(lambda e, hc=hc, par=par, dc=dc,
                         first=first, last=last, bank=bank: e.matmul(
                    bank[:, 0:TA], wt_buf[par][:, dc, :], xf[:, dc, :],
                    start=first, stop=last), wait=w0 if first else ())
            wt_tok[par] = mm
            w_u = [mm]
            if us_tok[par]:
                w_u.append(us_tok[par])
            tv_u = p.v(lambda e, hc=hc, par=par, bank=bank: e.tensor_scalar(
                out=u_sb[par][:], in0=bank[:, 0:TA],
                scalar1=C["beta_col"][:, hc:hc + 1],
                scalar2=None, op0=ALU.mult), wait=w_u)
            pb_tok[par] = tv_u
            p.drain("vector")
            tk_us = p.gd(lambda e, hc=hc, par=par, t0=t0: e.dma_start(
                out=u_s[hc, :, t0:t0 + TA], in_=u_sb[par][:]),
                wait=[tv_u], stream=f"a1us{par}")
            us_tok[par] = tk_us
        tile_done[0] = [tk_xns, p.tok("T"), p.tok("A"), p.tok("V")]
    st["prev_done"] = [p.tok("D_a1xn"), p.tok("D_a1us0"), p.tok("D_a1us1")]


def _sl2(buf, col0, n):
    base = buf[:, col0:col0 + 1]
    return bass.AP(base.tensor, base.offset, [list(base.ap[0]), [2, n]])


def emit_A1b(st, u_s, cx_s):
    p, cfg, sb, bufs, C = st["p"], st["cfg"], st["sb"], st["bufs"], st["C"]
    DC, S = cfg.DC, cfg.S
    NL = S.bit_length() - 1
    ub = [sb(f"a1b_ub{i}", [128, 2 * S]) for i in range(2)]
    off = [0]
    for l in range(1, NL + 1):
        off.append(off[-1] + (S >> (l - 1)))
    ld_tok = [(), ()]
    stv_tok = [(), ()]
    st_tok = [(), ()]
    prev_d = st.get("prev_done", [])

    for hc in range(DC):
        par = hc % 2
        b = ub[par]
        w_l = list(prev_d) if hc < 2 else []
        if st_tok[par]:
            w_l.append(st_tok[par])
        if stv_tok[par]:
            w_l.append(stv_tok[par])
        tk = p.ld(lambda e, hc=hc, par=par, b=b: e.dma_start(
            out=b[:, 0:S], in_=u_s[hc, :, :]), wait=w_l, stream=f"a1b{par}")
        ld_tok[par] = tk
        tv = None
        for l in range(1, NL + 1):
            nl = S >> l
            w = [tk] if l == 1 else []
            tv = p.v(lambda e, l=l, nl=nl, hc=hc, b=b: e.scalar_tensor_tensor(
                out=b[:, off[l]:off[l] + nl],
                in0=_sl2(b, off[l - 1], nl),
                scalar=C["Atab"][:, hc * 12 + (l - 1):hc * 12 + l],
                in1=_sl2(b, off[l - 1] + 1, nl),
                op0=ALU.mult, op1=ALU.add), wait=w)
            p.drain("vector")
        for l in range(NL, 0, -1):
            nl = S >> l
            if nl > 1:
                tv = p.v(lambda e, l=l, nl=nl, hc=hc, b=b: e.scalar_tensor_tensor(
                    out=_sl2(b, off[l - 1] + 2, nl - 1),
                    in0=b[:, off[l]:off[l] + nl - 1],
                    scalar=C["Atab"][:, hc * 12 + (l - 1):hc * 12 + l],
                    in1=_sl2(b, off[l - 1] + 2, nl - 1),
                    op0=ALU.mult, op1=ALU.add))
                p.drain("vector")
            tv = p.v(lambda e, l=l, nl=nl, b=b: e.tensor_copy(
                out=_sl2(b, off[l - 1] + 1, nl),
                in_=b[:, off[l]:off[l] + nl]))
            p.drain("vector")
        stv_tok[par] = tv
        tg = p.gd(lambda e, hc=hc, par=par, b=b: e.dma_start(
            out=cx_s[hc, :, :], in_=b[:, 0:S]), wait=[tv], stream=f"a1bs{par}")
        st_tok[par] = tg
    st["prev_done"] = [p.tok("D_a1bs0"), p.tok("D_a1bs1")]


def emit_A1c(st, X, xn_s, cx_s, a0_s):
    p, cfg, sb, bufs, C = st["p"], st["cfg"], st["sb"], st["bufs"], st["C"]
    D, DC, TA, S = cfg.D, cfg.DC, cfg.TA, cfg.S
    TC = TA // 128
    NT = S // TA
    pb, pw, pstat, ptr = st["pb"], st["pw"], st["pstat"], st["ptr"]
    pb_tok, pw_tok, pstat_tok, ptr_tok = st["pb_tok"], st["pw_tok"], st["pstat_tok"], st["ptr_tok"]
    W_in = st["W_in"]

    x_t = sb("a1c_x", [128, TC, D])
    xn_t = sb("a1c_xn", [128, TC, D])
    lx_t = sb("a1c_lx", [128, TC, D])
    xf = sb("a1c_xf", [128, DC, TA])
    cx = sb("a1c_cx", [128, DC, TA])
    lncx = sb("a1c_lncx", [128, DC, TA])
    af = sb("a1c_af", [128, DC, TA])
    at_t = xn_t
    g_t = sb("a1c_g", [128, TC, D])
    sq_sb = sb("a1c_sq", [128, TA])
    mu_b = sb("a1c_mu", [128, TA])
    ms_b = sb("a1c_ms", [128, TA])
    var_b = sb("a1c_var", [128, TA])
    rs_b = sb("a1c_rs", [128, TA])
    rsg_b = sb("a1c_rsg", [128, TA])
    tmp1 = sb("a1c_tmp1", [128, TA])
    for c in range(TC):
        mk_ln_bufs(sb, bufs, f"c{c}")
        mk_ln_bufs(sb, bufs, f"g{c}")
    wo_buf = [sb(f"a1c_wo{i}", [128, DC, 128]) for i in range(2)]
    wg_buf = [sb(f"a1c_wg{i}", [128, D]) for i in range(2)]
    wo_tok = [(), ()]
    wg_tok = [(), ()]

    prev_d = st.get("prev_done", [])
    tile_done = [list(prev_d)]

    for j in range(NT):
        t0 = j * TA
        tk_x = p.ld(lambda e, t0=t0: e.dma_start(
            out=x_t[:],
            in_=X[t0:t0 + TA, :].rearrange("(c p) d -> p c d", p=128)),
            wait=tile_done[0], stream="a1cx")
        tk_xn = p.ld(lambda e, t0=t0: e.dma_start(
            out=xn_t[:],
            in_=xn_s[t0:t0 + TA, :].rearrange("(c p) d -> p c d", p=128)),
            wait=tile_done[0], stream="a1cxn")
        tk_cx = p.ld(lambda e, t0=t0: e.dma_start(
            out=cx[:],
            in_=cx_s[0:DC, 0:128, t0:t0 + TA].transpose([1, 0, 2])),
            wait=tile_done[0], stream="a1ccx")
        t_ln1 = [emit_layernorm_T(p, bufs, xn_t[:, c, :], lx_t[:, c, :], D,
                                  cfg.eps_l0, [tk_xn], f"c{c}",
                                  gamma_bc=C["ln1_w_bc"][:], beta_bc=C["ln1_b_bc"][:])
                 for c in range(TC)]
        tv_lx = t_ln1[-1]
        for c in range(TC):
            for dc in range(DC):
                w = [t_ln1[c]]
                if ptr_tok[0]:
                    w.append(ptr_tok[0])
                tt = p.t(lambda e, c=c, dc=dc: e.transpose(
                    ptr[0:128, 0:128], lx_t[:, c, dc * 128:(dc + 1) * 128],
                    C["ident32"][:]), wait=w)
                ta_h = p.a(lambda e, c=c, dc=dc: e.copy(
                    out=xf[:, dc, c * 128:(c + 1) * 128], in_=ptr[0:128, 0:128]),
                    wait=[tt])
                ptr_tok[0] = ta_h
        tv_xf = ta_h
        last_sq_mm = None
        mm1 = None
        for hc in range(DC):
            w = [tk_cx]
            if hc == 0 and pstat_tok[0]:
                w.append(pstat_tok[0])
            if hc == 0 and pb_tok[2]:
                w.append(pb_tok[2])
            mm1 = p.t(lambda e, hc=hc, first=(hc == 0): e.matmul(
                pstat[:, 0:TA], C["ones32"][:],
                cx[:, hc, :], start=first, stop=(hc == DC - 1)),
                wait=w)
            ta_sq = p.a(lambda e, hc=hc: e.activation(
                out=sq_sb[:], in_=cx[:, hc, :], func=AF.Square),
                wait=[tk_cx] + ([last_sq_mm] if last_sq_mm else []))
            mm2 = p.t(lambda e, hc=hc, first=(hc == 0): e.matmul(
                pb[2][:, 0:TA], C["ones32"][:],
                sq_sb[:], start=first, stop=(hc == DC - 1)),
                wait=[ta_sq])
            last_sq_mm = mm2
        tv_mu = p.v(lambda e: e.tensor_scalar(
            out=mu_b[:], in0=pstat[:, 0:TA], scalar1=1.0 / D, scalar2=None,
            op0=ALU.mult), wait=[mm1, last_sq_mm])
        pstat_tok[0] = tv_mu
        tv_ms = p.v(lambda e: e.tensor_scalar(
            out=ms_b[:], in0=pb[2][:, 0:TA], scalar1=1.0 / D, scalar2=None,
            op0=ALU.mult))
        pb_tok[2] = tv_ms
        p.drain("vector")
        p.v(lambda e: e.tensor_tensor(out=var_b[:], in0=mu_b[:], in1=mu_b[:],
                                      op=ALU.mult))
        p.drain("vector")
        tv_var = p.v(lambda e: e.tensor_tensor(out=var_b[:], in0=ms_b[:], in1=var_b[:],
                                               op=ALU.subtract))
        p.drain("vector")
        ta_sd = p.a(lambda e: e.activation(out=rs_b[:], in_=var_b[:], func=AF.Sqrt,
                                           bias=float(cfg.eps_l0), scale=1.0),
                    wait=[tv_var])
        tv_rs = p.v(lambda e: e.reciprocal(out=rs_b[:], in_=rs_b[:]), wait=[ta_sd])
        p.drain("vector")
        tv_lnc = []
        for hc in range(DC):
            p.v(lambda e, hc=hc: e.tensor_scalar(
                out=rsg_b[:], in0=rs_b[:], scalar1=C["lnc_w_col"][:, hc:hc + 1],
                scalar2=None, op0=ALU.mult))
            p.drain("vector")
            p.v(lambda e, hc=hc: e.tensor_tensor(
                out=tmp1[:], in0=cx[:, hc, :], in1=mu_b[:], op=ALU.subtract))
            p.drain("vector")
            p.v(lambda e, hc=hc: e.tensor_tensor(
                out=tmp1[:], in0=tmp1[:], in1=rsg_b[:], op=ALU.mult))
            p.drain("vector")
            tv_t3 = p.v(lambda e, hc=hc: e.tensor_scalar(
                out=lncx[:, hc, :], in0=tmp1[:], scalar1=C["lnc_b_col"][:, hc:hc + 1],
                scalar2=None, op0=ALU.add))
            p.drain("vector")
            tv_lnc.append(tv_t3)
        ta_af = [None] * DC
        for dco in range(DC):
            par = dco % 2
            w_w = [wo_tok[par]] if wo_tok[par] else []
            tk_w = p.ld(lambda e, dco=dco, par=par: e.dma_start(
                out=wo_buf[par][:],
                in_=W_in["Wo"][:, dco * 128:(dco + 1) * 128].rearrange(
                    "(c p) d -> p c d", p=128)),
                wait=w_w, stream=f"a1cwo{par}")
            bank = pb[par]
            w0 = [tk_w, tv_lnc[-1]]
            if pb_tok[par]:
                w0.append(pb_tok[par])
            mm = None
            for hc in range(DC):
                first = (hc == 0)
                last = (hc == DC - 1)
                mm = p.t(lambda e, dco=dco, par=par, hc=hc,
                         first=first, last=last, bank=bank: e.matmul(
                    bank[:, 0:TA], wo_buf[par][:, hc, :], lncx[:, hc, :],
                    start=first, stop=last), wait=w0 if first else ())
            wo_tok[par] = mm
            ta_h = p.a(lambda e, dco=dco, bank=bank: e.copy(
                out=af[:, dco, :], in_=bank[:, 0:TA]), wait=[mm])
            pb_tok[par] = ta_h
            ta_af[dco] = ta_h
        tv_af_all = ta_h
        tv_blend = []
        tv_gp_c = [None] * TC
        for cp in range((TC + 1) // 2):
            grp = [2 * cp] + ([2 * cp + 1] if 2 * cp + 1 < TC else [])
            c0 = grp[0]
            c1 = grp[1] if len(grp) > 1 else grp[0]
            mm = None
            DH2 = min(512, D)
            NHALF = D // DH2
            halves = {}
            for h in range(NHALF):
                halves[(c0, h)] = pb[h][:, 0:DH2]
                if len(grp) > 1:
                    halves[(c1, h)] = pw[:, h * DH2:(h + 1) * DH2]
            for kc in range(2 * DC):
                parw = kc % 2
                w_w = [wg_tok[parw]] if wg_tok[parw] else []
                tk_w = p.ld(lambda e, kc=kc, parw=parw: e.dma_start(
                    out=wg_buf[parw][:],
                    in_=W_in["Wg"][kc * 128:(kc + 1) * 128, :]),
                    wait=w_w, stream=f"a1cwg{parw}")
                stat_src = xf if kc < DC else af
                scc = kc if kc < DC else kc - DC
                w0 = [tk_w, tv_af_all, tv_xf]
                if kc == 0:
                    for g in (pb_tok[0], pb_tok[1], pw_tok[0]):
                        if g:
                            w0.append(g)
                kc_first = True
                for cc in grp:
                    for half in range(NHALF):
                        first = (kc == 0)
                        last = (kc == 2 * DC - 1)
                        st_ap = stat_src[:, scc, cc * 128:(cc + 1) * 128]
                        out_ap = halves[(cc, half)]
                        w_mm = (w0 if kc == 0 else [tk_w]) if kc_first else ()
                        kc_first = False
                        mm = p.t(lambda e, st_ap=st_ap, out_ap=out_ap, parw=parw,
                                 half=half, first=first, last=last,
                                 DH2=DH2:
                                 e.matmul(out_ap, st_ap,
                                          wg_buf[parw][:, half * DH2:(half + 1) * DH2],
                                          start=first, stop=last),
                                 wait=w_mm)
                wg_tok[parw] = mm
            for cc in grp:
                for h in range(NHALF):
                    tv_gp = p.v(lambda e, cc=cc, h=h, halves=halves, DH2=DH2:
                                e.scalar_tensor_tensor(
                        out=g_t[:, cc, h * DH2:(h + 1) * DH2], in0=halves[(cc, h)],
                        scalar=1.0, in1=C["bg_bc"][:, h * DH2:(h + 1) * DH2],
                        op0=ALU.mult, op1=ALU.add),
                        wait=[mm] if (cc == c0 and h == 0) else ())
                tv_gp_c[cc] = tv_gp
            for h in range(NHALF):
                pb_tok[h] = tv_gp_c[c0]
            if len(grp) > 1:
                pw_tok[0] = tv_gp_c[c1]
        for dco in range(DC):
            for c in range(TC):
                w_tr = [ta_af[dco]]
                if ptr_tok[0]:
                    w_tr.append(ptr_tok[0])
                tt2 = p.t(lambda e, dco=dco, c=c: e.transpose(
                    ptr[0:128, 0:128], af[:, dco, c * 128:(c + 1) * 128],
                    C["ident32"][:]), wait=w_tr)
                w_at = [tt2]
                if dco == 0 and c == 0:
                    w_at.append(tv_lx)
                tv_at = p.v(lambda e, dco=dco, c=c: e.tensor_copy(
                    out=at_t[:, c, dco * 128:(dco + 1) * 128], in_=ptr[0:128, 0:128]),
                    wait=w_at)
                ptr_tok[0] = tv_at
        tv_at_all = tv_at
        for c in range(TC):
            tki = emit_layernorm_T(p, bufs, g_t[:, c, :], g_t[:, c, :], D,
                                   cfg.eps_l0, [tv_gp_c[c]], f"g{c}",
                                   gamma_bc=C["lng_w_bc"][:], beta_bc=C["lng_b_bc"][:])
            ta_g = p.a(lambda e, c=c: e.activation(
                out=g_t[:, c, :], in_=g_t[:, c, :], func=AF.Sigmoid), wait=[tki])
            p.drain("scalar")
            tv_b1 = p.v(lambda e, c=c: e.tensor_tensor(
                out=at_t[:, c, :], in0=at_t[:, c, :], in1=lx_t[:, c, :],
                op=ALU.subtract), wait=[ta_g, tv_at_all])
            p.drain("vector")
            tv_b2 = p.v(lambda e, c=c: e.tensor_tensor(
                out=at_t[:, c, :], in0=g_t[:, c, :], in1=at_t[:, c, :], op=ALU.mult))
            p.drain("vector")
            tv_b3 = p.v(lambda e, c=c: e.tensor_tensor(
                out=at_t[:, c, :], in0=at_t[:, c, :], in1=lx_t[:, c, :], op=ALU.add))
            p.drain("vector")
            tv_b4 = p.v(lambda e, c=c: e.tensor_tensor(
                out=at_t[:, c, :], in0=at_t[:, c, :], in1=x_t[:, c, :], op=ALU.add),
                wait=[tk_x])
            p.drain("vector")
            tv_blend.append(tv_b4)
        tk_a0 = p.gd(lambda e, t0=t0: e.dma_start(
            out=a0_s[t0:t0 + TA, :].rearrange("(c p) d -> p c d", p=128),
            in_=at_t[:]), wait=tv_blend, stream="a1a0")
        tile_done[0] = [tv_blend[-1], tk_a0, p.tok("T"), p.tok("A"), p.tok("V")]
    st["prev_done"] = [p.tok("D_a1xn"), p.tok("D_a1a0")]



def emit_ffn_phase(st, xn_src, attn_src, dst, L, FWp):
    p, cfg, sb, bufs, C = st["p"], st["cfg"], st["sb"], st["bufs"], st["C"]
    D, I, DC, IC, TF, S = cfg.D, cfg.I, cfg.DC, cfg.IC, cfg.TF, cfg.S
    TC = TF // 128
    NT = S // TF
    NQ = (I + 511) // 512
    pb, pw, ptr = st["pb"], st["pw"], st["ptr"]
    pb_tok, pw_tok, ptr_tok = st["pb_tok"], st["pw_tok"], st["ptr_tok"]
    ready = st.get("pp_ready", {})
    pfx = f"f{L}"

    A = dict(
        xn_t=sb(pfx + "_xn", [128, TC, D]),
        xq=sb(pfx + "_xq", [128, D]),
        xqF=sb(pfx + "_xqF", [128, DC, TF]),
        sg_sb=[sb(f"{pfx}_sg{i}", [128, 512]) for i in range(2)],
        h_nq=[sb(f"{pfx}_h{i}", [128, 512]) for i in range(2)],
        hq=[sb(f"{pfx}_hq{i}", [128, 512]) for i in range(2)],
        hqT=[sb(f"{pfx}_hqT{i}", [128, 4, 128]) for i in range(2)],
        wg_b=[sb(f"{pfx}_wg{i}", [128, DC, 512]) for i in range(2)],
        wu_b=[sb(f"{pfx}_wu{i}", [128, DC, 512]) for i in range(2)],
        wd_b=[sb(f"{pfx}_wd{i}", [128, 4, D]) for i in range(2)],
        a0_sl=[sb(f"{pfx}_a00", [128, D])],
        out_sl=[sb(f"{pfx}_out0", [128, D])],
    )
    for tag, n in ((pfx + "x", D), (pfx + "h", 512)):
        for nm, shape, dt in [
                (f"qax_{tag}", [128, n], F32), (f"qsg_{tag}", [128, n], F32),
                (f"qqa_{tag}", [128, n], F32),
                (f"qqb_{tag}", [128, n], F32), (f"qbm_{tag}", [128, n // 16], F32),
                (f"qsc_{tag}", [128, n // 16], F32), (f"qrcp_{tag}", [128, n // 16], F32)]:
            bufs[nm] = sb(nm, shape, dt)
    xn_t = A["xn_t"]
    acc = xn_t

    wgu_tok = [(), ()]
    wd_tok = [(), ()]
    wld_tok = [None, None]
    a0_tok = [(), ()]
    out_tok = [(), ()]
    prev_d = st.get("prev_done", [])
    tile_done = [st.get("pp_done", [])[:] + prev_d]

    gkey, ukey, dkey = (L, "g"), (L, "u"), (L, "d")
    rg = [ready[k] for k in (f"L{L}g",) if k in ready]
    for j in range(NT):
        t0 = j * TF
        tk_xn = p.ld(lambda e, t0=t0: e.dma_start(
            out=xn_t[:], in_=xn_src[t0:t0 + TF, :].rearrange("(c p) d -> p c d", p=128)),
            wait=tile_done[0] + rg, stream=pfx + "xn")
        xq_state = {"tprev": tk_xn, "tt_last": None}
        tv_xqf_tc = [None] * TC

        def emit_xq(tc):
            tq = emit_quant(p, bufs, xn_t[:, tc, :], None, None, None,
                            D, pfx + "x",
                            [tk_xn, xq_state["tprev"]] +
                            ([xq_state["tt_last"]] if xq_state["tt_last"] else []),
                            out32=A["xq"][:, 0:D])
            te = None
            for dc in range(DC):
                w_tr = [tq]
                if ptr_tok[0]:
                    w_tr.append(ptr_tok[0])
                tt = p.t(lambda e, dc=dc: e.transpose(
                    ptr[0:128, 0:128], A["xq"][:, dc * 128:(dc + 1) * 128],
                    C["ident32"][:]), wait=w_tr)
                xq_state["tt_last"] = tt
                te = p.a(lambda e, dc=dc, tc=tc: e.copy(
                    out=A["xqF"][:, dc, tc * 128:(tc + 1) * 128], in_=ptr[0:128, 0:128]),
                    wait=[tt])
                ptr_tok[0] = te
            xq_state["tprev"] = te
            tv_xqf_tc[tc] = te

        emit_xq(0)
        NIT = NQ * TC
        gu_done = [None] * NIT
        h_done = [None] * NIT
        q_done = [None] * NIT
        tr_done = [None] * NIT
        dn_done = [None] * NIT
        ev_done = [None] * NIT
        tkd_of = [None] * NQ

        def emit_gu(it):
            nq, tc = divmod(it, TC)
            wq = min(512, I - nq * 512)
            par = nq % 2
            if tc == 0:
                w_w = [wgu_tok[par]] if wgu_tok[par] else []
                tk_g = p.ld(lambda e, nq=nq, par=par, wq=wq: e.dma_start(
                    out=A["wg_b"][par][:, :, 0:wq],
                    in_=FWp[gkey][0:DC, 0:128, nq * 512:nq * 512 + wq].transpose([1, 0, 2])),
                    wait=w_w, stream=pfx + f"wg{par}")
                tk_u = p.ld(lambda e, nq=nq, par=par, wq=wq: e.dma_start(
                    out=A["wu_b"][par][:, :, 0:wq],
                    in_=FWp[ukey][0:DC, 0:128, nq * 512:nq * 512 + wq].transpose([1, 0, 2])),
                    wait=[], stream=pfx + f"wu{par}")
                w_w = [wd_tok[par]] if wd_tok[par] else []
                NS = wq // 128
                tk_d = p.ld(lambda e, nq=nq, par=par, NS=NS: e.dma_start(
                    out=A["wd_b"][par][:, 0:NS, :],
                    in_=FWp[dkey][nq * 4:nq * 4 + NS, 0:128, :].transpose([1, 0, 2])),
                    wait=w_w, stream=pfx + f"wd{par}")
                wld_tok[par] = (tk_g, tk_u)
                tkd_of[nq] = tk_d
            tk_g, tk_u = wld_tok[par]
            bp = it % 2
            gb = pb[2 * bp]
            ub = pb[2 * bp + 1]
            w0 = [tk_g, tk_u, tv_xqf_tc[tc]]
            for gg in (pb_tok[2 * bp], pb_tok[2 * bp + 1]):
                if gg:
                    w0.append(gg)
            mmg = mmu = None
            for kc in range(DC):
                stat = A["xqF"][:, kc, tc * 128:(tc + 1) * 128]
                fg = (kc == 0)
                lg = (kc == DC - 1)
                mmg = p.t(lambda e, stat=stat, kc=kc, par=par,
                          wq=wq, gb=gb, fg=fg, lg=lg: e.matmul(
                    gb[:, 0:wq], stat, A["wg_b"][par][:, kc, 0:wq],
                    start=fg, stop=lg), wait=w0 if fg else ())
                mmu = p.t(lambda e, stat=stat, kc=kc, par=par,
                          wq=wq, ub=ub, fg=fg, lg=lg: e.matmul(
                    ub[:, 0:wq], stat, A["wu_b"][par][:, kc, 0:wq],
                    start=fg, stop=lg))
            wgu_tok[par] = mmu
            gu_done[it] = mmu
            return gb, ub

        def emit_quant_h(it, gb, ub):
            nq, tc = divmod(it, TC)
            wq = min(512, I - nq * 512)
            bp = it % 2
            w_sg = [gu_done[it]]
            if h_done[it - 1] if it else None:
                w_sg.append(h_done[it - 1])
            ta_sg = p.a(lambda e, gb=gb, wq=wq, bp=bp: e.activation(
                out=A["sg_sb"][bp][:, 0:wq], in_=gb[:, 0:wq], func=AF.Silu),
                wait=w_sg)
            tv_h = p.v(lambda e, ub=ub, wq=wq, bp=bp: e.tensor_tensor(
                out=A["h_nq"][bp][:, 0:wq], in0=A["sg_sb"][bp][:, 0:wq],
                in1=ub[:, 0:wq], op=ALU.mult), wait=[ta_sg])
            pb_tok[2 * bp] = tv_h
            pb_tok[2 * bp + 1] = tv_h
            h_done[it] = tv_h
            w_q = [tv_h]
            if it >= 2 and tr_done[it - 2]:
                w_q.append(tr_done[it - 2])
            tqh = emit_quant(p, bufs, A["h_nq"][bp][:, 0:wq], None, None, None,
                             wq, pfx + "h", w_q, out32=A["hq"][bp][:, 0:wq])
            q_done[it] = tqh

        def emit_tr_dn_ev(it):
            nq, tc = divmod(it, TC)
            wq = min(512, I - nq * 512)
            NS = wq // 128
            par = nq % 2
            bp = it % 2
            te = None
            for ns in range(NS):
                w_tr = [q_done[it]]
                if ptr_tok[0]:
                    w_tr.append(ptr_tok[0])
                if ns == 0 and it >= 2 and dn_done[it - 2]:
                    w_tr.append(dn_done[it - 2])
                tt = p.t(lambda e, ns=ns, bp=bp: e.transpose(
                    ptr[0:128, 0:128], A["hq"][bp][:, ns * 128:(ns + 1) * 128],
                    C["ident32"][:]), wait=w_tr)
                te = p.a(lambda e, ns=ns, bp=bp: e.copy(
                    out=A["hqT"][bp][:, ns, :], in_=ptr[0:128, 0:128]), wait=[tt])
                ptr_tok[0] = te
            tr_done[it] = te
            w0d = [tkd_of[nq], te]
            if it and ev_done[it - 1]:
                w0d.append(ev_done[it - 1])
            mmd = None
            DH2 = min(512, D)
            NH2 = D // DH2
            for ns in range(NS):
                fg = (ns == 0)
                lg = (ns == NS - 1)
                for half in range(NH2):
                    mmd = p.t(lambda e, ns=ns, par=par, bp=bp,
                              half=half, fg=fg, lg=lg, DH2=DH2: e.matmul(
                        pw[:, half * DH2:(half + 1) * DH2], A["hqT"][bp][:, ns, :],
                        A["wd_b"][par][:, ns, half * DH2:(half + 1) * DH2],
                        start=fg, stop=lg),
                        wait=w0d if (fg and half == 0) else ())
            wd_tok[par] = mmd
            dn_done[it] = mmd
            if nq == 0:
                ev = p.v(lambda e, tc=tc: e.tensor_copy(
                    out=acc[:, tc, :], in_=pw[:, 0:D]), wait=[mmd])
            else:
                ev = p.v(lambda e, tc=tc: e.tensor_tensor(
                    out=acc[:, tc, :], in0=acc[:, tc, :], in1=pw[:, 0:D],
                    op=ALU.add), wait=[mmd] + ([ev_tok[tc][nq - 1]] if ev_tok[tc][nq - 1] else []))
            pw_tok[0] = ev
            ev_done[it] = ev
            ev_tok[tc][nq] = ev

        ev_tok = [[None] * NQ for _ in range(TC)]
        gu_banks = [None] * NIT
        gu_banks[0] = emit_gu(0)
        if TC > 1:
            emit_xq(1)
        for it in range(NIT):
            emit_quant_h(it, *gu_banks[it])
            if it + 2 < TC:
                emit_xq(it + 2)
            if it + 1 < NIT:
                gu_banks[it + 1] = emit_gu(it + 1)
            emit_tr_dn_ev(it)
        st_toks = []
        for tc in range(TC):
            par = 0
            w_a = [a0_tok[par]] if a0_tok[par] else []
            if j == 0 and tc < 2:
                w_a = w_a + prev_d
            tk_a = p.ld(lambda e, t0=t0, tc=tc: e.dma_start(
                out=A["a0_sl"][0][:],
                in_=attn_src[t0 + tc * 128:t0 + (tc + 1) * 128, :]),
                wait=w_a, stream=pfx + f"a0{par}")
            w_o = [tk_a, ev_tok[tc][NQ - 1]]
            if out_tok[par]:
                w_o.append(out_tok[par])
            tv_o = p.v(lambda e, tc=tc, par=par: e.tensor_tensor(
                out=A["out_sl"][par][:], in0=acc[:, tc, :], in1=A["a0_sl"][par][:],
                op=ALU.add), wait=w_o)
            p.drain("vector")
            tk_o = p.gd(lambda e, t0=t0, tc=tc, par=par: e.dma_start(
                out=dst[t0 + tc * 128:t0 + (tc + 1) * 128, :],
                in_=A["out_sl"][par][:]), wait=[tv_o], stream=pfx + f"o{par}")
            a0_tok[par] = tv_o
            out_tok[par] = tk_o
            st_toks.append(tk_o)
        tile_done[0] = st_toks + [p.tok("V"), p.tok("A"), p.tok("T")]
    st["prev_done"] = [p.tok(f"D_{pfx}o0")]



def emit_B0(st, x1_s, xn2_s, xnF_s):
    p, cfg, sb, bufs, C = st["p"], st["cfg"], st["sb"], st["bufs"], st["C"]
    D, DC, S = cfg.D, cfg.DC, cfg.S
    TA = 512
    TC = TA // 128
    NT = S // TA
    ptr = st["ptr"]
    ptr_tok = st["ptr_tok"]
    x1_t = sb("b0_x1", [128, TC, D])
    xn2_t = sb("b0_xn2", [128, TC, D])
    xnF = sb("b0_xnF", [128, DC, TA])
    for c in range(TC):
        mk_ln_bufs(sb, bufs, f"b0{c}")
    tile_done = [list(st.get("prev_done", []))]
    for j in range(NT):
        t0 = j * TA
        tk_x = p.ld(lambda e, t0=t0: e.dma_start(
            out=x1_t[:],
            in_=x1_s[t0:t0 + TA, :].rearrange("(c p) d -> p c d", p=128)),
            wait=tile_done[0], stream="b0x")
        t_bln = [emit_layernorm_T(p, bufs, x1_t[:, c, :], xn2_t[:, c, :], D,
                                  cfg.eps_blk, [tk_x], f"b0{c}") for c in range(TC)]
        tk_xns = p.gd(lambda e, t0=t0: e.dma_start(
            out=xn2_s[t0:t0 + TA, :].rearrange("(c p) d -> p c d", p=128),
            in_=xn2_t[:]), wait=t_bln, stream="b0xn")
        tv_l = None
        for c in range(TC):
            for dc in range(DC):
                w = [t_bln[c]]
                if ptr_tok[0]:
                    w.append(ptr_tok[0])
                tt = p.t(lambda e, c=c, dc=dc: e.transpose(
                    ptr[0:128, 0:128], xn2_t[:, c, dc * 128:(dc + 1) * 128],
                    C["ident32"][:]), wait=w)
                tv_l = p.a(lambda e, c=c, dc=dc: e.copy(
                    out=xnF[:, dc, c * 128:(c + 1) * 128], in_=ptr[0:128, 0:128]),
                    wait=[tt])
                ptr_tok[0] = tv_l
        tk_xf = p.gd(lambda e, t0=t0: e.dma_start(
            out=xnF_s[0:DC, 0:128, t0:t0 + TA].transpose([1, 0, 2]),
            in_=xnF[:]), wait=[tv_l], stream="b0xf")
        tile_done[0] = [tk_xns, tk_xf, p.tok("T"), p.tok("V"), p.tok("A")]
    st["prev_done"] = [p.tok("D_b0xn"), p.tok("D_b0xf")]


def emit_B1(st, xnF_s, av_s):
    p, cfg, sb, bufs, C = st["p"], st["cfg"], st["sb"], st["bufs"], st["C"]
    D, DC, S, TB = cfg.D, cfg.DC, cfg.S, cfg.TB
    NH = cfg.NH
    CPT = TB // 126
    NCH = cfg.NCH
    NTB = (NCH + CPT - 1) // CPT
    W = cfg.W
    pb, pw, pstat, ptr = st["pb"], st["pw"], st["pstat"], st["ptr"]
    pb_tok, pw_tok, pstat_tok, ptr_tok = st["pb_tok"], st["pw_tok"], st["pstat_tok"], st["ptr_tok"]
    W_in = st["W_in"]

    xf_t = [sb(f"b1_xf{i}", [128, DC, 512]) for i in range(2)]
    qT = [sb(f"b1_qT{i}", [128, 512]) for i in range(2)]
    kT = [sb(f"b1_kT{i}", [128, 512]) for i in range(2)]
    vF = [sb(f"b1_vF{i}", [128, 512]) for i in range(2)]
    vT = [sb(f"b1_vT{i}", [128, 2, 65]) for i in range(CPT)]
    e_sb = [sb(f"b1_e{i}", [128, 2, 252]) for i in range(CPT)]
    av32 = [sb(f"b1_av32{i}", [128, 252]) for i in range(CPT)]
    r64 = [sb(f"b1_r64{i}", [128, 504]) for i in range(CPT)]
    av_hp = [sb(f"b1_av{i}", [128, S]) for i in range(2)]
    wq_b = [[sb(f"b1_w{i}_{q}", [128, DC, 128]) for q in range(3)] for i in range(2)]
    wld_tok = [(), ()]
    avz_tok = [(), ()]
    xfu_tok = [(), ()]
    sc_tok = [(), ()]
    avp_tok = [(), ()]
    vT_tok = [()] * CPT
    e_tok = [()] * CPT
    av32_tok = [()] * CPT

    pwh_tok = [pw_tok[0], pw_tok[0]]
    tvo = p.v(lambda e: e.memset(vT[0][:, :, 64:65], 1.0))
    for i in range(1, CPT):
        tvo = p.v(lambda e, i=i: e.memset(vT[i][:, :, 64:65], 1.0))
    tz0 = p.v(lambda e: e.memset(av_hp[0][:], 0.0))
    tz1 = p.v(lambda e: e.memset(av_hp[1][:], 0.0))
    avz_tok[0] = tz0
    avz_tok[1] = tz1

    prev_d = list(st.get("prev_done", []))

    def emit_qkv(hp, k):
        hpar = hp % 2
        t0 = k * TB
        c0 = k * CPT
        ncc = min(CPT, NCH - c0)
        LL = min(((126 * (ncc - 1) + 128) + 127) // 128 * 128, S - t0)
        xpar = k % 2
        w_xf = [xfu_tok[xpar]] if xfu_tok[xpar] else []
        tk_xf = p.ld(lambda e, t0=t0, LL=LL, xpar=xpar: e.dma_start(
            out=xf_t[xpar][:, :, 0:LL],
            in_=xnF_s[0:DC, 0:128, t0:t0 + LL].transpose([1, 0, 2])),
            wait=w_xf, stream=f"b1xf{xpar}")
        part_toks = {}
        mm = None
        for part, qi in (("q", 0), ("k", 1), ("v", 2)):
            bank = pw[:, (qi % 2) * 512:(qi % 2) * 512 + 512]
            w0 = [tk_w[qi], tk_xf]
            g = pwh_tok[qi % 2]
            if g:
                w0.append(g)
            mm = None
            for dc in range(DC):
                first = (dc == 0)
                last = (dc == DC - 1)
                mm = p.t(lambda e, qi=qi, dc=dc, hpar=hpar, xpar=xpar,
                         first=first, last=last, bank=bank, LL=LL: e.matmul(
                    bank[:, 0:LL], wq_b[hpar][qi][:, dc, :],
                    xf_t[xpar][:, dc, 0:LL],
                    start=first, stop=last), wait=w0 if first else ())
            dsth = {"q": qT, "k": kT, "v": vF}[part][xpar]
            ta_h = p.a(lambda e, dsth=dsth, bank=bank, LL=LL: e.copy(
                out=dsth[:, 0:LL], in_=bank[:, 0:LL]), wait=[mm])
            pwh_tok[qi % 2] = ta_h
            part_toks[part] = ta_h
        xfu_tok[xpar] = mm
        return part_toks

    def emit_chains(hp, k, part_toks):
        hpar = hp % 2
        av = av_hp[hpar]
        t0 = k * TB
        c0 = k * CPT
        ncc = min(CPT, NCH - c0)
        xpar = k % 2
        geom = []
        for cc in range(ncc):
            cg = c0 + cc
            geom.append((cc, 126 * cc, min(128, S - 126 * cg),
                         min(63, W - 63 * cg)))
        exp_tok = [None] * ncc
        for cc, ko, KW, NW in geom:
            NQw = 4 * NW
            w_tr = [part_toks["v"]]
            if ptr_tok[0]:
                w_tr.append(ptr_tok[0])
            tt = p.t(lambda e, ko=ko, KW=KW, xpar=xpar: e.transpose(
                ptr[0:KW, 0:128], vF[xpar][:, ko:ko + KW], C["ident32"][:]),
                wait=w_tr)
            w_v = [tt]
            if vT_tok[cc]:
                w_v.append(vT_tok[cc])
            te1 = p.a(lambda e, KW=KW, cc=cc: e.copy(
                out=vT[cc][0:KW, 0, 0:64], in_=ptr[0:KW, 0:64]), wait=w_v)
            te2 = p.a(lambda e, KW=KW, cc=cc: e.copy(
                out=vT[cc][0:KW, 1, 0:64], in_=ptr[0:KW, 64:128]))
            ptr_tok[0] = te2
            vT_tok[cc] = te2
            for ha in range(2):
                sbank = pb[ha]
                w0 = [part_toks["q"], part_toks["k"]]
                if sc_tok[ha]:
                    w0.append(sc_tok[ha])
                q_gather = _qgather(qT[xpar], ha, ko, NW)
                mm2 = p.t(lambda e, ha=ha, ko=ko, KW=KW, mov=q_gather, NQw=NQw,
                          sbank=sbank, xpar=xpar: e.matmul(
                    sbank[0:KW, 0:NQw],
                    kT[xpar][64 * ha:64 * ha + 64, ko:ko + KW],
                    mov, start=True, stop=True), wait=w0)
                w_m = [mm2]
                if ha == 0 and e_tok[cc]:
                    w_m.append(e_tok[cc])
                tv_m = p.v(lambda e, ha=ha, KW=KW, NQw=NQw, NW=NW, sbank=sbank,
                           cc=cc: e.scalar_tensor_tensor(
                    out=e_sb[cc][0:KW, ha, 0:NQw].rearrange("p (j w) -> p j w", w=NW),
                    in0=sbank[0:KW, 0:NQw].rearrange(
                        "p (j w) -> p j w", w=NW),
                    scalar=float(D) ** -0.5,
                    in1=C["amask"][0:KW, :].rearrange("p (j w) -> p j w", j=4)[:, :, 0:NW],
                    op0=ALU.mult, op1=ALU.add), wait=w_m)
                sc_tok[ha] = tv_m
                ta_e = p.a(lambda e, ha=ha, KW=KW, NQw=NQw, cc=cc: e.activation(
                    out=e_sb[cc][0:KW, ha, 0:NQw], in_=e_sb[cc][0:KW, ha, 0:NQw],
                    func=AF.Exp), wait=[tv_m])
                exp_tok[cc] = ta_e
        s_tok = [None] * ncc
        for cc, ko, KW, NW in geom:
            NQw = 4 * NW
            for ha in range(2):
                abank = pb[2 + ha]
                w0a = [exp_tok[cc], vT_tok[cc]]
                if avp_tok[ha]:
                    w0a.append(avp_tok[ha])
                mma = p.t(lambda e, ha=ha, KW=KW, NQw=NQw, abank=abank, cc=cc:
                          e.matmul(
                    abank[0:65, 0:NQw], vT[cc][0:KW, ha, 0:65],
                    e_sb[cc][0:KW, ha, 0:NQw], start=True, stop=True),
                    wait=w0a)
                w_s = [mma]
                if ha == 0 and av32_tok[cc]:
                    w_s.append(av32_tok[cc])
                tv_s = p.v(lambda e, ha=ha, NQw=NQw, abank=abank, cc=cc:
                           e.tensor_copy(
                    out=r64[cc][64:65, ha * NQw:(ha + 1) * NQw],
                    in_=abank[64:65, 0:NQw]), wait=w_s)
                tv_av = p.v(lambda e, ha=ha, NQw=NQw, abank=abank, cc=cc:
                            e.tensor_copy(
                    out=av32[cc][64 * ha:64 * ha + 64, 0:NQw],
                    in_=abank[0:64, 0:NQw]))
                avp_tok[ha] = tv_av
                s_tok[cc] = tv_av
        for cc, ko, KW, NW in geom:
            NQw = 4 * NW
            w_rb = [s_tok[cc]]
            if pstat_tok[0]:
                w_rb.append(pstat_tok[0])
            tbm = p.t(lambda e, NQw=NQw, cc=cc: e.matmul(
                pstat[:, 0:2 * NQw],
                C["ones32"][64:65, 0:128],
                r64[cc][64:65, 0:2 * NQw],
                start=True, stop=True),
                wait=w_rb)
            tv_rec = p.v(lambda e, NQw=NQw: e.reciprocal(
                out=pstat[:, 0:2 * NQw], in_=pstat[:, 0:2 * NQw]), wait=[tbm])
            tv_n1 = p.v(lambda e, NQw=NQw, cc=cc: e.tensor_tensor(
                out=av32[cc][0:64, 0:NQw], in0=av32[cc][0:64, 0:NQw],
                in1=pstat[0:64, 0:NQw], op=ALU.mult), wait=[tv_rec])
            tv_n = p.v(lambda e, NQw=NQw, cc=cc: e.tensor_tensor(
                out=av32[cc][64:128, 0:NQw], in0=av32[cc][64:128, 0:NQw],
                in1=pstat[64:128, NQw:2 * NQw], op=ALU.mult))
            pstat_tok[0] = tv_n
            cmb = None
            for jq in range(4):
                base = av[:, t0 + ko + jq:t0 + ko + jq + 1]
                sl_out = bass.AP(base.tensor, base.offset,
                                 [list(base.ap[0]), [2, NW]])
                w_c = [avz_tok[hpar]] if (k == 0 and cc == 0 and jq == 0
                                          and avz_tok[hpar]) else ()
                cmb = p.v(lambda e, sl_out=sl_out, jq=jq, NW=NW, cc=cc:
                          e.tensor_tensor(
                    out=sl_out, in0=sl_out,
                    in1=av32[cc][:, jq * NW:(jq + 1) * NW], op=ALU.add), wait=w_c)
            av32_tok[cc] = cmb
            e_tok[cc] = cmb

    for hp in range(NH // 2):
        hpar = hp % 2
        w_w = list(prev_d) if hp < 2 else []
        if wld_tok[hpar]:
            w_w.append(wld_tok[hpar])
        tk_w = {}
        for qi, fc in ((0, hp), (1, DC + hp), (2, 2 * DC + hp)):
            tk_w[qi] = p.ld(lambda e, qi=qi, fc=fc, hpar=hpar: e.dma_start(
                out=wq_b[hpar][qi][:],
                in_=W_in["Wqkv"][:, fc * 128:(fc + 1) * 128].rearrange(
                    "(c p) f -> p c f", p=128)),
                wait=w_w if qi == 0 else [], stream=f"b1w{hpar}{qi}")
        pend = None
        for k in range(NTB):
            pt = emit_qkv(hp, k)
            if pend is not None:
                emit_chains(hp, k - 1, pend)
            pend = pt
        emit_chains(hp, NTB - 1, pend)
        wld_tok[hpar] = xfu_tok[0]
        if xfu_tok[1]:
            wld_tok[hpar] = xfu_tok[1]
        w_st = [p.tok("V")]
        tk_av = p.gd(lambda e, hp=hp, hpar=hpar: e.dma_start(
            out=av_s[hp, :, :], in_=av_hp[hpar][:]), wait=w_st, stream=f"b1av{hpar}")
        if hp + 2 < NH // 2:
            tz = p.v(lambda e, hpar=hpar: e.memset(av_hp[hpar][:], 0.0),
                     wait=[tk_av])
            avz_tok[hpar] = tz
    pw_tok[0] = pwh_tok[1] or pwh_tok[0]
    st["prev_done"] = [p.tok("D_b1av0"), p.tok("D_b1av1")]


def emit_B2(st, x1_s, av_s, a1_s, MT):
    p, cfg, sb, bufs, C = st["p"], st["cfg"], st["sb"], st["bufs"], st["C"]
    D, DC, S = cfg.D, cfg.DC, cfg.S
    NH = cfg.NH
    TA = 512
    TC = TA // 128
    NT = S // TA
    pb, pw = st["pb"], st["pw"]
    pb_tok, pw_tok = st["pb_tok"], st["pw_tok"]
    W_in = st["W_in"]

    avb = sb("b2_av", [128, NH // 2, TA])
    x1_t = sb("b2_x1", [128, TC, D])
    mt_t = sb("b2_mt", [128, TC, 1])
    wo_b = [sb(f"b2_wo{i}", [128, D]) for i in range(2)]
    out_sl = [sb(f"b2_out{i}", [128, D]) for i in range(2)]
    wo_tok = [(), ()]
    out_tok = [(), ()]
    DH2 = min(512, D)
    NH2 = D // DH2
    prev_d = list(st.get("prev_done", []))
    tile_done = [prev_d]

    for j in range(NT):
        t0 = j * TA
        tk_av = p.ld(lambda e, t0=t0: e.dma_start(
            out=avb[:],
            in_=av_s[0:NH // 2, 0:128, t0:t0 + TA].transpose([1, 0, 2])),
            wait=tile_done[0], stream="b2av")
        tk_x = p.ld(lambda e, t0=t0: e.dma_start(
            out=x1_t[:],
            in_=x1_s[t0:t0 + TA, :].rearrange("(c p) d -> p c d", p=128)),
            wait=tile_done[0], stream="b2x")
        tk_mt = p.ld(lambda e, t0=t0: e.dma_start(
            out=mt_t[:],
            in_=MT[t0:t0 + TA, :].rearrange("(c p) d -> p c d", p=128),
            allow_slow_non_contiguous=True),
            wait=tile_done[0], stream="b2mt")
        for tcp in range(TC // 2):
            tcs = [2 * tcp, 2 * tcp + 1]
            banks = {tcs[0]: [pb[h][:, 0:DH2] for h in range(NH2)],
                     tcs[1]: [pw[:, h * DH2:(h + 1) * DH2] for h in range(NH2)]}
            mm = None
            for ccw in range(DC):
                parw = ccw % 2
                w_w = [wo_tok[parw]] if wo_tok[parw] else []
                tk_w = p.ld(lambda e, ccw=ccw, parw=parw: e.dma_start(
                    out=wo_b[parw][:],
                    in_=W_in["Wout"][ccw * 128:(ccw + 1) * 128, :]),
                    wait=w_w, stream=f"b2wo{parw}")
                kc_first = True
                for tc in tcs:
                    for half in range(NH2):
                        first = (ccw == 0)
                        last = (ccw == DC - 1)
                        w_mm = ([tk_w, tk_av] +
                                [g for g in (pb_tok[0], pb_tok[1], pw_tok[0]) if g]
                                ) if (ccw == 0 and kc_first) else (
                                [tk_w] if kc_first else ())
                        kc_first = False
                        mm = p.t(lambda e, ccw=ccw, parw=parw, tc=tc,
                                 half=half, first=first,
                                 last=last, banks=banks, DH2=DH2: e.matmul(
                            banks[tc][half][:, :],
                            avb[:, ccw, tc * 128:(tc + 1) * 128],
                            wo_b[parw][:, half * DH2:(half + 1) * DH2],
                            start=first, stop=last), wait=w_mm)
                wo_tok[parw] = mm
            for i, tc in enumerate(tcs):
                par = tc % 2
                w_o = [mm, tk_mt, tk_x]
                if out_tok[par]:
                    w_o.append(out_tok[par])
                tv_o1 = None
                for half in range(NH2):
                    tv_o1 = p.v(lambda e, tc=tc, half=half, banks=banks,
                                par=par, DH2=DH2: e.scalar_tensor_tensor(
                        out=out_sl[par][:, half * DH2:(half + 1) * DH2],
                        in0=C["bout_bc"][:, half * DH2:(half + 1) * DH2],
                        scalar=mt_t[:, tc, 0:1],
                        in1=banks[tc][half][:, :],
                        op0=ALU.mult, op1=ALU.add), wait=w_o if half == 0 else ())
                tv_o2 = p.v(lambda e, tc=tc, par=par: e.tensor_tensor(
                    out=out_sl[par][:], in0=out_sl[par][:],
                    in1=x1_t[:, tc, :], op=ALU.add))
                if i == 0:
                    for h in range(NH2):
                        pb_tok[h] = tv_o2
                else:
                    pw_tok[0] = tv_o2
                tk_o = p.gd(lambda e, t0=t0, tc=tc, par=par: e.dma_start(
                    out=a1_s[t0 + tc * 128:t0 + (tc + 1) * 128, :],
                    in_=out_sl[par][:]), wait=[tv_o2], stream=f"b2o{par}")
                out_tok[par] = tk_o
        tile_done[0] = [p.tok("V"), p.tok("T"), p.tok("D_b2o0"), p.tok("D_b2o1")]
    st["prev_done"] = [p.tok("D_b2o0"), p.tok("D_b2o1")]


def _qgather(qT, ha, ko, NW):
    base = qT[64 * ha:64 * ha + 64, ko:ko + 2]
    rs = base.ap[0][0]
    import bass_rust
    return bass.AP(base.tensor, base.offset,
                   [[rs, 64], [2, 2], [1, 2], [2, NW]])


from concourse.bass_utils import run_bass_kernel_spmd

_NC_CACHE = {}


def _get_nc():
    if "nc" not in _NC_CACHE:
        cfg = Cfg(D=1024, I=2816, S=4096, TA=512, TF=512, TB=504)
        _NC_CACHE["cfg"] = cfg
        _NC_CACHE["nc"] = build(cfg, phases=("P", "A1", "A2", "B", "C"))
    return _NC_CACHE["nc"], _NC_CACHE["cfg"]


def kernel(L0_attn_x, L0_attn_Wt, L0_attn_Wo, L0_attn_Wg, L0_attn_bg,
           L0_attn_ln1_w, L0_attn_ln1_b, L0_attn_lnc_w, L0_attn_lnc_b,
           L0_attn_lng_w, L0_attn_lng_b, L0_ffn_w_gate, L0_ffn_w_up,
           L0_ffn_w_down, L1_attn_w_qkv, L1_attn_w_out, L1_attn_b_out,
           L1_ffn_w_gate, L1_ffn_w_up, L1_ffn_w_down):
    nc, cfg = _get_nc()
    hcst = host_consts(cfg)
    f32 = lambda a: np.ascontiguousarray(np.asarray(a, dtype=np.float32))
    shared = {
        "Wt": f32(L0_attn_Wt), "Wo": f32(L0_attn_Wo), "Wg": f32(L0_attn_Wg),
        "Wqkv": f32(L1_attn_w_qkv), "Wout": f32(L1_attn_w_out),
        "L0_wgate": f32(L0_ffn_w_gate), "L0_wup": f32(L0_ffn_w_up),
        "L0_wdown": f32(L0_ffn_w_down),
        "L1_wgate": f32(L1_ffn_w_gate), "L1_wup": f32(L1_ffn_w_up),
        "L1_wdown": f32(L1_ffn_w_down),
        "ln1_w_bc": bcast128(f32(L0_attn_ln1_w)),
        "ln1_b_bc": bcast128(f32(L0_attn_ln1_b)),
        "lng_w_bc": bcast128(f32(L0_attn_lng_w)),
        "lng_b_bc": bcast128(f32(L0_attn_lng_b)),
        "bg_bc": bcast128(f32(L0_attn_bg)),
        "bout_bc": bcast128(f32(L1_attn_b_out)),
        "lnc_w_col": col128(f32(L0_attn_lnc_w), cfg.DC),
        "lnc_b_col": col128(f32(L0_attn_lnc_b), cfg.DC),
        "ident32": hcst["ident32"], "ones32": hcst["ones32"],
        "amask": hcst["amask"], "mt_vec": hcst["mt_vec"],
        "beta_col": hcst["beta_col"], "Atab": hcst["Atab"],
        "fconst": hcst["fconst"],
    }
    xs = np.ascontiguousarray(np.asarray(L0_attn_x, dtype=np.float32))
    n_cores = xs.shape[0]
    in_maps = [dict(shared, x_in=np.ascontiguousarray(xs[i]))
               for i in range(n_cores)]
    res = run_bass_kernel_spmd(nc, in_maps, core_ids=list(range(n_cores)))
    out = np.stack([np.asarray(r["out"], dtype=np.float32)
                    for r in res.results], axis=0)
    return out
